# version 02
# baseline (speedup 1.0000x reference)
import contextlib
import numpy as np
import concourse.bass as bass
import concourse.mybir as mybir

F32 = mybir.dt.float32
BF16 = mybir.dt.bfloat16
AF = mybir.ActivationFunctionType
ALU = mybir.AluOpType
AX = mybir.AxisListType.X

P = 128
NDT = 8
NC = 512


class BB:

    def __init__(self, nc, stack):
        self.nc = nc
        self.stack = stack
        self.block = stack.enter_context(nc.Block())
        self.sems = {}
        self.cnt = {}

    def sem(self, name):
        if name not in self.sems:
            self.sems[name] = self.stack.enter_context(self.nc.semaphore(name))
            self.cnt[name] = 0
        return self.sems[name]

    def inc(self, instr, name, n=1):
        instr.then_inc(self.sem(name), n)
        self.cnt[name] += n
        return instr

    def val(self, name):
        self.sem(name)
        return self.cnt[name]

    def wait(self, eng, name, v=None):
        eng.wait_ge(self.sem(name), self.cnt[name] if v is None else v)

    def V(self, f): self.block.vector(f)
    def T(self, f): self.block.tensor(f)
    def S(self, f): self.block.scalar(f)
    def G(self, f): self.block.gpsimd(f)
    def Y(self, f): self.block.sync(f)

    def barrier(self, engines="vtsgy"):
        names = {"v": "bar_v", "t": "bar_t", "s": "bar_s", "g": "bar_g", "y": "bar_y"}
        eng = {"v": self.V, "t": self.T, "s": self.S, "g": self.G, "y": self.Y}
        for e in engines:
            def _f(E, nm=names[e]):
                self.inc(E.nop(), nm)
            eng[e](_f)
        vals = {e: self.val(names[e]) for e in engines}
        for e in engines:
            def _f(E, me=e):
                for o in engines:
                    if o != me:
                        E.wait_ge(self.sem(names[o]), vals[o])
            eng[e](_f)


def col_groups(w, base=0):
    out = []
    a = 0
    while a < w:
        b = min(a + 512 - ((base + a) % 512), w)
        out.append((base + a, base + b))
        a = b
    return out


def dslice(T, j, a, b, W):
    return T[:, j * W + a: j * W + b]



def emit_ln_stats(bb, X, XW, xa, xb, sq, SQW, ps_stat, rows, onescol, onesrow,
                  ps_mb, ps_rb, tag):
    w = xb - xa
    Wr = rows.shape[1] // 4

    def _s(S):
        bb.wait(S, "v")
        for j in range(NDT):
            ins = S.activation(out=dslice(sq, j, xa, xb, SQW),
                               in_=dslice(X, j, xa, xb, XW), func=AF.Square)
        bb.inc(ins, "s")
    bb.S(_s)

    def _t(T):
        bb.wait(T, "s")
        bb.wait(T, "v")
        for (a, b) in col_groups(w, xa):
            for j in range(NDT):
                T.matmul(ps_stat[0:1, a:b], onescol[:], dslice(X, j, a, b, XW),
                         start=(j == 0), stop=(j == NDT - 1))
        for (a, b) in col_groups(w, xa):
            for j in range(NDT):
                ins = T.matmul(ps_stat[32:33, a:b], onescol[:],
                               dslice(sq, j, a, b, SQW),
                               start=(j == 0), stop=(j == NDT - 1))
        bb.inc(ins, "t")
    bb.T(_t)

    def _v(V):
        bb.wait(V, "t")
        V.tensor_scalar_mul(rows[0:1, 0:w], ps_stat[0:1, xa:xb], 1.0 / 1024.0)
        V.tensor_scalar_mul(rows[0:1, Wr:Wr + w], ps_stat[32:33, xa:xb], 1.0 / 1024.0)
        V.tensor_tensor(rows[0:1, 2 * Wr:2 * Wr + w], rows[0:1, 0:w], rows[0:1, 0:w], ALU.mult)
        V.tensor_tensor(rows[0:1, Wr:Wr + w], rows[0:1, Wr:Wr + w],
                        rows[0:1, 2 * Wr:2 * Wr + w], ALU.subtract)
        ins = V.tensor_scalar_add(rows[0:1, Wr:Wr + w], rows[0:1, Wr:Wr + w], 1e-5)
        bb.inc(ins, "v")
    bb.V(_v)

    def _s2(S):
        bb.wait(S, "v")
        ins = S.activation(out=rows[0:1, 2 * Wr:2 * Wr + w], in_=rows[0:1, Wr:Wr + w],
                           func=AF.Sqrt)
        bb.inc(ins, "s")
    bb.S(_s2)

    def _v2(V):
        bb.wait(V, "s")
        ins = V.reciprocal(rows[0:1, 3 * Wr:3 * Wr + w], rows[0:1, 2 * Wr:2 * Wr + w])
        bb.inc(ins, "v")
    bb.V(_v2)

    def _t2(T):
        bb.wait(T, "v")
        for (a, b) in col_groups(w, xa):
            T.matmul(ps_mb[:, a:b], onesrow[0:1, :], rows[0:1, a - xa:b - xa],
                     start=True, stop=True)
            ins = T.matmul(ps_rb[:, a:b], onesrow[0:1, :],
                           rows[0:1, 3 * Wr + a - xa:3 * Wr + b - xa],
                           start=True, stop=True)
        bb.inc(ins, "t")
    bb.T(_t2)


def emit_ln_apply(bb, X, XW, OUT, OUTW, xa, xb, oa, ps_mb, ps_rb, mid=None):
    w = xb - xa

    def _v(V):
        bb.wait(V, "t")
        if mid is None:
            MID, MIDW, ma = OUT, OUTW, oa
        else:
            MID, MIDW = mid
            ma = 0
        for j in range(NDT):
            V.tensor_tensor(dslice(MID, j, ma, ma + w, MIDW),
                            dslice(X, j, xa, xb, XW), ps_mb[:, xa:xb], ALU.subtract)
        for j in range(NDT):
            ins = V.tensor_tensor(dslice(OUT, j, oa, oa + w, OUTW),
                                  dslice(MID, j, ma, ma + w, MIDW), ps_rb[:, xa:xb], ALU.mult)
        bb.inc(ins, "v")
    bb.V(_v)


def emit_rms(bb, X, XW, OUT, OUTW, xa, xb, oa, sq, SQW, ps_stat, rows,
             onescol, ones32row, ps_bc, gp1, tag, tmp=None):
    w = xb - xa
    Wr = rows.shape[1] // 4

    def _s(S):
        bb.wait(S, "v")
        for j in range(NDT):
            ins = S.activation(out=dslice(sq, j, xa, xb, SQW),
                               in_=dslice(X, j, xa, xb, XW), func=AF.Square)
        bb.inc(ins, "s")
    bb.S(_s)

    def _t(T):
        bb.wait(T, "s")
        for (a, b) in col_groups(w, xa):
            for j in range(NDT):
                ins = T.matmul(ps_stat[0:1, a:b], onescol[:], dslice(sq, j, a, b, SQW),
                               start=(j == 0), stop=(j == NDT - 1))
        bb.inc(ins, "t")
    bb.T(_t)

    def _s2(S):
        bb.wait(S, "t")
        ins = S.activation(out=rows[0:1, 0:w], in_=ps_stat[0:1, xa:xb], func=AF.Sqrt)
        bb.inc(ins, "s")
    bb.S(_s2)

    def _v(V):
        bb.wait(V, "s")
        bb.inc(V.tensor_scalar_max(rows[0:1, Wr:Wr + w], rows[0:1, 0:w], 1e-12), "v")
        bb.wait(V, "v")
        ins = V.reciprocal(rows[0:1, 2 * Wr:2 * Wr + w], rows[0:1, Wr:Wr + w])
        bb.inc(ins, "v")
    bb.V(_v)

    def _t2(T):
        bb.wait(T, "v")
        for (a, b) in col_groups(w, xa):
            ins = T.matmul(ps_bc[:, a:b], ones32row[0:1, :],
                           rows[0:1, 2 * Wr + a - xa:2 * Wr + b - xa], start=True, stop=True)
        bb.inc(ins, "t")
    bb.T(_t2)

    def _v2(V):
        bb.wait(V, "t")
        if tmp is None:
            MID, MIDW, ma = OUT, OUTW, oa
        else:
            MID, MIDW = tmp
            ma = 0
        for j in range(NDT):
            V.tensor_tensor(dslice(MID, j, ma, ma + w, MIDW),
                            dslice(X, j, xa, xb, XW), ps_bc[:, xa:xb], ALU.mult)
        for j in range(NDT):
            ins = V.tensor_scalar_mul(dslice(OUT, j, oa, oa + w, OUTW),
                                      dslice(MID, j, ma, ma + w, MIDW), gp1[:, j:j + 1])
        bb.inc(ins, "v")
    bb.V(_v2)


def emit_mm(bb, ps, wsb, WW, rhs, RHSW, ra, rb_, out_et, n_kt=NDT, wait=("v", "s"),
            extra_lhsT=None, extra_rhs=None):
    def _t(T):
        for s in wait:
            bb.wait(T, s)
        for (a, b) in col_groups(rb_ - ra, ra):
            for kt in range(n_kt):
                ins = T.matmul(ps[:, a:b],
                               wsb[:, kt * WW + 128 * out_et: kt * WW + 128 * out_et + 128],
                               dslice(rhs, kt, a, b, RHSW),
                               start=(kt == 0),
                               stop=(kt == n_kt - 1 and extra_lhsT is None))
            if extra_lhsT is not None:
                ins = T.matmul(ps[:, a:b], extra_lhsT, extra_rhs(a, b),
                               start=False, stop=True)
        bb.inc(ins, "t")
    bb.T(_t)



def emit_stageA(bb, nc, st, io, n):
    e = st.enter_context
    W = 517
    WL = 515
    WO = 513

    xc = e(nc.sbuf_tensor("A_xc", [P, NDT * WL], F32))
    sq = e(nc.sbuf_tensor("A_sq", [P, NDT * WL], F32))
    ln0 = e(nc.sbuf_tensor("A_ln0", [P, NDT * WL], F32))
    r0 = e(nc.sbuf_tensor("A_r0", [P, NDT * W], F32))
    x1 = e(nc.sbuf_tensor("A_x1", [P, NDT * WO], F32))
    hid = e(nc.sbuf_tensor("A_hid", [P, NDT * WO], F32))
    at = e(nc.sbuf_tensor("A_a", [P, NDT * WO], F32))
    ht = e(nc.sbuf_tensor("A_h", [P, NDT * WO], F32))
    ybf = e(nc.sbuf_tensor("A_ybf", [P, NDT * WO], BF16))
    r1bf = e(nc.sbuf_tensor("A_r1bf", [P, NDT * WO], BF16))
    zer = e(nc.sbuf_tensor("A_zer", [P, WO], F32))
    rows = e(nc.sbuf_tensor("A_rows", [1, 4 * WL], F32))
    wpw = e(nc.sbuf_tensor("A_wpw", [P, NDT * 1024], BF16))
    wgru = e(nc.sbuf_tensor("A_wgru", [P, NDT * 2048], BF16))
    veca = e(nc.sbuf_tensor("A_veca", [P, 64], F32))
    gp1c = e(nc.sbuf_tensor("A_gp1c", [P, NDT], F32))
    gp1g = e(nc.sbuf_tensor("A_gp1g", [P, NDT], F32))
    cb = e(nc.sbuf_tensor("A_cb", [P, NDT], F32))
    hcar = e(nc.sbuf_tensor("A_hcar", [P, NDT], F32))
    gcar = e(nc.sbuf_tensor("A_gcar", [P, NDT], F32))
    consts = e(nc.sbuf_tensor("A_consts", [P, 4], F32))
    onescol = e(nc.sbuf_tensor("A_ones", [P, 1], F32))
    onesrow = e(nc.sbuf_tensor("A_onesr", [1, P], F32))
    ones32 = e(nc.sbuf_tensor("A_ones32", [1, P], F32))

    ps_stat = e(nc.psum_tensor("A_pstat", [33, 1024], F32))
    ps_b1 = e(nc.psum_tensor("A_psb1", [P, 1024], F32))
    ps_b2 = e(nc.psum_tensor("A_psb2", [P, 1024], F32))
    ps_mm = e(nc.psum_tensor("A_psmm", [P, 1024], F32))

    C_CG, C_GG, C_DB, C_W0, C_W1, C_W2, C_PB = 0, 8, 16, 24, 32, 40, 48

    def _g(Gp):
        for j in range(NDT):
            bb.inc(Gp.dma_start(out=wpw[:, j * 1024:(j + 1) * 1024],
                                in_=io["pwT"][j * 128:(j + 1) * 128, :]), "gd", 16)
        for j in range(NDT):
            bb.inc(Gp.dma_start(out=wgru[:, j * 2048:(j + 1) * 2048],
                                in_=io["gruT"][j * 128:(j + 1) * 128, :]), "gd", 16)
    bb.G(_g)

    def _y(Y):
        bb.inc(Y.dma_start(out=veca[:, 0:56], in_=io["veca"][:]), "yd", 16)
        bb.inc(Y.dma_start(out=consts[:], in_=io["consts"][:]), "yd", 16)
    bb.Y(_y)
    nc.const_aps.register(F32, -1.0, consts[:, 0:1])
    nc.const_aps.register(F32, 0.125, consts[:, 1:2])
    nc.const_aps.register(F32, 1e-6, consts[:, 2:3])

    def _v(V):
        bb.wait(V, "yd")
        V.memset(onescol[:], 1.0)
        V.memset(onesrow[:], 1.0)
        V.memset(ones32[:], 32.0)
        V.memset(zer[:], 0.0)
        V.memset(hcar[:], 0.0)
        V.memset(gcar[:], 1.0)
        V.tensor_scalar_add(gp1c[:], veca[:, C_CG:C_CG + 8], 1.0)
        ins = V.tensor_scalar_add(gp1g[:], veca[:, C_GG:C_GG + 8], 1.0)
        bb.inc(ins, "v")
    bb.V(_v)

    dwbf = e(nc.sbuf_tensor("A_dwbf", [P, NDT], BF16))

    def _v2(V):
        bb.wait(V, "v")
        ins = V.tensor_copy(dwbf[:], veca[:, C_DB:C_DB + 8])
        bb.inc(ins, "v")
    bb.V(_v2)

    def _t_fix(T):
        bb.wait(T, "v")
        bb.wait(T, "gd")
        for et in range(NDT):
            for j in range(NDT):
                ins = T.matmul(ps_mm[:, 512 + et:512 + et + 1],
                               wpw[:, j * 1024 + 128 * et: j * 1024 + 128 * et + 128],
                               dwbf[:, j:j + 1],
                               start=(j == 0), stop=(j == NDT - 1))
        bb.inc(ins, "t")
    bb.T(_t_fix)

    def _v3(V):
        bb.wait(V, "t")
        ins = V.tensor_tensor(cb[:], ps_mm[:, 512:520], veca[:, C_PB:C_PB + 8], ALU.add)
        bb.inc(ins, "v")
    bb.V(_v3)

    def _tw(T):
        bb.wait(T, "gd")
        bb.wait(T, "yd")
    bb.T(_tw)
    bb.barrier("vtsy")

    nchunks = n // NC
    for c in range(nchunks):
        if c == 0:
            xa, xw = 0, WL
            fr_a, fr_w = 0, WL
            oa_ln = 2
            wo = WO
            ocol = 0
        else:
            xa, xw = 3 + c * NC - 0, NC
            fr_a, fr_w = 2, NC
            oa_ln = 0
            wo = NC
            ocol = 1 + c * NC

        def _yl(Y, c=c, xa=xa, xw=xw):
            if c > 0:
                bb.wait(Y, "v")
            for j in range(NDT):
                bb.inc(Y.dma_start(out=dslice(xc, j, 0, xw, WL),
                                   in_=io["x_t"][j * 128:(j + 1) * 128, xa:xa + xw]), "yd", 16)
        bb.Y(_yl)

        def _vw(V):
            bb.wait(V, "yd")
            bb.inc(V.nop(), "v")
        bb.V(_vw)

        if c >= 1:
            src = 513 if c == 1 else 512
            def _vc(V, src=src):
                for j in range(NDT):
                    ins = V.tensor_copy(dslice(r0, j, 0, 2, W), dslice(r0, j, src, src + 2, W))
                bb.inc(ins, "v")
            bb.V(_vc)

        emit_ln_stats(bb, xc, WL, 0, xw, sq, WL, ps_stat, rows, onescol, onesrow,
                      ps_b1, ps_b2, f"ln0_{c}")
        emit_ln_apply(bb, xc, WL, ln0, WL, 0, xw, 0, ps_b1, ps_b2)
        emit_rms(bb, ln0, WL, r0, W, 0, xw, fr_a, sq, WL, ps_stat, rows,
                 onescol, ones32, ps_b1, gp1c, f"r0_{c}")

        def _vconv(V, wo=wo, fr_a=fr_a, c=c):
            bb.wait(V, "v")
            base = 0 if c == 0 else 0
            for j in range(NDT):
                V.tensor_scalar_mul(dslice(hid, j, 0, wo, WO),
                                    dslice(r0, j, base, base + wo, W), veca[:, C_W0 + j:C_W0 + j + 1])
            for j in range(NDT):
                V.scalar_tensor_tensor(dslice(hid, j, 0, wo, WO),
                                       dslice(r0, j, base + 1, base + 1 + wo, W),
                                       veca[:, C_W1 + j:C_W1 + j + 1],
                                       dslice(hid, j, 0, wo, WO), ALU.mult, ALU.add)
            for j in range(NDT):
                ins = V.scalar_tensor_tensor(dslice(ybf, j, 0, wo, WO),
                                             dslice(r0, j, base + 2, base + 2 + wo, W),
                                             veca[:, C_W2 + j:C_W2 + j + 1],
                                             dslice(hid, j, 0, wo, WO), ALU.mult, ALU.add)
            bb.inc(ins, "v")
        bb.V(_vconv)

        for et in range(NDT):
            emit_mm(bb, ps_mm, wpw, 1024, ybf, WO, 0, wo, et)
            def _vev(V, et=et, wo=wo, oa_ln=oa_ln):
                bb.wait(V, "t")
                ins = V.scalar_tensor_tensor(dslice(x1, et, 0, wo, WO),
                                             ps_mm[:, 0:wo], cb[:, et:et + 1],
                                             dslice(ln0, et, oa_ln, oa_ln + wo, WL),
                                             ALU.add, ALU.add)
                bb.inc(ins, "v")
            bb.V(_vev)

        emit_rms(bb, x1, WO, r1bf, WO, 0, wo, 0, sq, WL, ps_stat, rows,
                 onescol, ones32, ps_b1, gp1g, f"r1_{c}", tmp=(hid, WO))

        for et in range(NDT):
            emit_mm(bb, ps_mm, wgru, 2048, r1bf, WO, 0, wo, et)
            def _vh(V, et=et, wo=wo):
                bb.wait(V, "t")
                ins = V.tensor_copy(dslice(hid, et, 0, wo, WO), ps_mm[:, 0:wo])
                bb.inc(ins, "v")
            bb.V(_vh)
        for et in range(NDT):
            emit_mm(bb, ps_mm, wgru, 2048, r1bf, WO, 0, wo, et + 8)
            def _sg(S, et=et, wo=wo):
                bb.wait(S, "t")
                S.activation(out=dslice(at, et, 0, wo, WO), in_=ps_mm[:, 0:wo],
                             func=AF.Sigmoid, scale=-1.0)
                S.activation(out=dslice(sq, et, 0, wo, WL), in_=ps_mm[:, 0:wo],
                             func=AF.Sigmoid)
                ins = S.activation(out=dslice(ln0, et, 0, wo, WL),
                                   in_=dslice(hid, et, 0, wo, WO), func=AF.Sigmoid)
                bb.inc(ins, "s")
            bb.S(_sg)
            def _vb(V, et=et, wo=wo):
                bb.wait(V, "s")
                V.tensor_scalar_min(dslice(ln0, et, 0, wo, WL),
                                    dslice(ln0, et, 0, wo, WL), 0.5)
                V.tensor_scalar_max(dslice(hid, et, 0, wo, WO),
                                    dslice(hid, et, 0, wo, WO), 0.0)
                V.tensor_tensor(dslice(ln0, et, 0, wo, WL), dslice(ln0, et, 0, wo, WL),
                                dslice(hid, et, 0, wo, WO), ALU.add)
                ins = V.tensor_tensor(dslice(sq, et, 0, wo, WL), dslice(sq, et, 0, wo, WL),
                                      dslice(ln0, et, 0, wo, WL), ALU.mult)
                bb.inc(ins, "v")
            bb.V(_vb)

        def _vscan(V, c=c, wo=wo):
            bb.wait(V, "v")
            sa = 1 if c == 0 else 0
            if c == 0:
                for j in range(NDT):
                    V.memset(dslice(ht, j, 0, 1, WO), 0.0)
                    V.memset(dslice(ln0, j, 0, 1, WL), 1.0)
            for j in range(NDT):
                V.tensor_tensor_scan(out=dslice(ht, j, sa, wo, WO),
                                     data0=dslice(at, j, sa, wo, WO),
                                     data1=dslice(sq, j, sa, wo, WL),
                                     initial=hcar[:, j:j + 1], op0=ALU.mult, op1=ALU.add)
            for j in range(NDT):
                V.tensor_tensor_scan(out=dslice(ln0, j, sa, wo, WL),
                                     data0=dslice(at, j, sa, wo, WO),
                                     data1=zer[:, sa:wo],
                                     initial=gcar[:, j:j + 1], op0=ALU.mult, op1=ALU.add)
            for j in range(NDT):
                V.tensor_copy(hcar[:, j:j + 1], dslice(ht, j, wo - 1, wo, WO))
                V.tensor_copy(gcar[:, j:j + 1], dslice(ln0, j, wo - 1, wo, WL))
            for j in range(NDT):
                ins = V.tensor_tensor(dslice(ht, j, 0, wo, WO), dslice(ht, j, 0, wo, WO),
                                      dslice(x1, j, 0, wo, WO), ALU.add)
            bb.inc(ins, "v")
        bb.V(_vscan)

        def _yst(Y, wo=wo, ocol=ocol):
            bb.wait(Y, "v")
            for j in range(NDT):
                bb.inc(Y.dma_start(out=io["x2p_t"][j * 128:(j + 1) * 128, ocol:ocol + wo],
                                   in_=dslice(ht, j, 0, wo, WO)), "st", 16)
            for j in range(NDT):
                bb.inc(Y.dma_start(out=io["g_t"][j * 128:(j + 1) * 128, ocol:ocol + wo],
                                   in_=dslice(ln0, j, 0, wo, WL)), "st", 16)
        bb.Y(_yst)

        bb.barrier("vtsy")

    def _yf(Y):
        ins = Y.dma_start(out=io["hfin"][:], in_=hcar[:])
        bb.inc(ins, "st", 16)
        bb.wait(Y, "st")
    bb.Y(_yf)


def build_A(n):
    nc = bass.Bass("TRN2", target_bir_lowering=False, detect_race_conditions=False)
    io = {}
    io["x_t"] = nc.dram_tensor("x_t", [1024, 3 + n], F32, kind="ExternalInput").ap()
    io["pwT"] = nc.dram_tensor("pwT", [1024, 1024], F32, kind="ExternalInput").ap()
    io["gruT"] = nc.dram_tensor("gruT", [1024, 2048], F32, kind="ExternalInput").ap()
    io["veca"] = nc.dram_tensor("veca", [128, 56], F32, kind="ExternalInput").ap()
    io["consts"] = nc.dram_tensor("consts", [128, 4], F32, kind="ExternalInput").ap()
    io["x2p_t"] = nc.dram_tensor("x2p_t", [1024, 1 + n], F32, kind="ExternalOutput").ap()
    io["g_t"] = nc.dram_tensor("g_t", [1024, 1 + n], F32, kind="ExternalOutput").ap()
    io["hfin"] = nc.dram_tensor("hfin", [128, 8], F32, kind="ExternalOutput").ap()
    with contextlib.ExitStack() as st:
        bb = BB(nc, st)
        emit_stageA(bb, nc, st, io, n)
    return nc



def emit_ln_apply_res(bb, X, XW, OUT, OUTW, RES, RESW, xa, xb, oa, ra,
                      ps_mb, ps_rb, mid, MIDW):
    w = xb - xa

    def _v(V):
        bb.wait(V, "t")
        for j in range(NDT):
            V.tensor_tensor(dslice(mid, j, 0, w, MIDW),
                            dslice(X, j, xa, xb, XW), ps_mb[:, 0:w], ALU.subtract)
        for j in range(NDT):
            V.tensor_tensor(dslice(mid, j, 0, w, MIDW),
                            dslice(mid, j, 0, w, MIDW), ps_rb[:, 0:w], ALU.mult)
        for j in range(NDT):
            ins = V.tensor_tensor(dslice(OUT, j, oa, oa + w, OUTW),
                                  dslice(mid, j, 0, w, MIDW),
                                  dslice(RES, j, ra, ra + w, RESW), ALU.add)
            bb.inc(ins, "v")
    bb.V(_v)


NCB = 256

def emit_stageB1(bb, nc, st, io, n):
    e = st.enter_context
    W = NCB
    NHT = 32

    xq = e(nc.sbuf_tensor("B_xq", [P, NDT * W], F32))
    gq = e(nc.sbuf_tensor("B_gq", [P, NDT * W], F32))
    sq = e(nc.sbuf_tensor("B_sq", [P, NDT * W], F32))
    x3 = e(nc.sbuf_tensor("B_x3", [P, NDT * W], F32))
    xr = e(nc.sbuf_tensor("B_xr", [P, NDT * W], F32))
    r2bf = e(nc.sbuf_tensor("B_r2bf", [P, NDT * W], BF16))
    h1bf = e(nc.sbuf_tensor("B_h1bf", [P, NHT * W], BF16))
    w1sb = e(nc.sbuf_tensor("B_w1", [P, NDT * 4096], BF16))
    w2sb = e(nc.sbuf_tensor("B_w2", [P, NHT * 1024], BF16))
    rows = e(nc.sbuf_tensor("B_rows", [1, 4 * W], F32))
    vecb = e(nc.sbuf_tensor("B_vecb", [P, 48], F32))
    gp1f = e(nc.sbuf_tensor("B_gp1f", [P, NDT], F32))
    hin = e(nc.sbuf_tensor("B_hin", [P, NDT], F32))
    consts = e(nc.sbuf_tensor("B_consts", [P, 4], F32))
    onescol = e(nc.sbuf_tensor("B_ones", [P, 1], F32))
    onesrow = e(nc.sbuf_tensor("B_onesr", [1, P], F32))
    ones32 = e(nc.sbuf_tensor("B_ones32", [1, P], F32))

    ps_stat = e(nc.psum_tensor("B_pstat", [33, 512], F32))
    ps_b1 = e(nc.psum_tensor("B_psb1", [P, 512], F32))
    ps_b2 = e(nc.psum_tensor("B_psb2", [P, 512], F32))
    ps_mm = e(nc.psum_tensor("B_psmm", [P, 512], F32))

    nc.const_aps.register(F32, -1.0, consts[:, 0:1])
    nc.const_aps.register(F32, 0.125, consts[:, 1:2])
    nc.const_aps.register(F32, 1e-6, consts[:, 2:3])

    C_FG, C_B1, C_B2 = 0, 8, 40

    def _g(Gp):
        for j in range(NDT):
            bb.inc(Gp.dma_start(out=w1sb[:, j * 4096:(j + 1) * 4096],
                                in_=io["ff1T"][j * 128:(j + 1) * 128, :]), "gd", 16)
        for j in range(NHT):
            bb.inc(Gp.dma_start(out=w2sb[:, j * 1024:(j + 1) * 1024],
                                in_=io["ff2T"][j * 128:(j + 1) * 128, :]), "gd", 16)
    bb.G(_g)

    def _y(Y):
        bb.inc(Y.dma_start(out=vecb[:], in_=io["vecb"][:]), "yd", 16)
        bb.inc(Y.dma_start(out=hin[:], in_=io["h_in"][:]), "yd", 16)
        bb.inc(Y.dma_start(out=consts[:], in_=io["consts"][:]), "yd", 16)
    bb.Y(_y)

    def _v(V):
        bb.wait(V, "yd")
        V.memset(onescol[:], 1.0)
        V.memset(onesrow[:], 1.0)
        V.memset(ones32[:], 32.0)
        ins = V.tensor_scalar_add(gp1f[:], vecb[:, C_FG:C_FG + 8], 1.0)
        bb.inc(ins, "v")
    bb.V(_v)

    def _tw(T):
        bb.wait(T, "gd")
        bb.wait(T, "yd")
    bb.T(_tw)
    bb.barrier("vtsy")

    chunks = [(0, 2, 1, 0)]
    for c in range(n // NCB):
        chunks.append((1 + c * NCB, 3 + c * NCB, NCB, 1 + c * NCB))

    for (pcol, rcol, wo, ocol) in chunks:
        def _yl(Y, pcol=pcol, wo=wo):
            for j in range(NDT):
                bb.inc(Y.dma_start(out=dslice(xq, j, 0, wo, W),
                                   in_=io["x2p_t"][j * 128:(j + 1) * 128, pcol:pcol + wo],
                                   allow_slow_non_contiguous=(wo == 1)), "yd", 16)
            for j in range(NDT):
                bb.inc(Y.dma_start(out=dslice(gq, j, 0, wo, W),
                                   in_=io["g_t"][j * 128:(j + 1) * 128, pcol:pcol + wo],
                                   allow_slow_non_contiguous=(wo == 1)), "yd", 16)
        bb.Y(_yl)

        def _vf(V, wo=wo):
            bb.wait(V, "yd")
            for j in range(NDT):
                ins = V.scalar_tensor_tensor(dslice(xq, j, 0, wo, W),
                                             dslice(gq, j, 0, wo, W), hin[:, j:j + 1],
                                             dslice(xq, j, 0, wo, W), ALU.mult, ALU.add)
            bb.inc(ins, "v")
        bb.V(_vf)

        emit_rms(bb, xq, W, r2bf, W, 0, wo, 0, sq, W, ps_stat, rows,
                 onescol, ones32, ps_b1, gp1f, "r2", tmp=(gq, W))

        for ht in range(NHT):
            emit_mm(bb, ps_mm, w1sb, 4096, r2bf, W, 0, wo, ht)
            def _sg(S, ht=ht, wo=wo):
                bb.wait(S, "t")
                ins = S.activation(out=dslice(h1bf, ht, 0, wo, W), in_=ps_mm[:, 0:wo],
                                   func=AF.Gelu, bias=vecb[:, C_B1 + ht:C_B1 + ht + 1])
                bb.inc(ins, "s")
            bb.S(_sg)

        for et in range(NDT):
            emit_mm(bb, ps_mm, w2sb, 1024, h1bf, W, 0, wo, et, n_kt=NHT)
            def _vd(V, et=et, wo=wo):
                bb.wait(V, "t")
                ins = V.scalar_tensor_tensor(dslice(x3, et, 0, wo, W),
                                             ps_mm[:, 0:wo], vecb[:, C_B2 + et:C_B2 + et + 1],
                                             dslice(xq, et, 0, wo, W), ALU.add, ALU.add)
                bb.inc(ins, "v")
            bb.V(_vd)

        def _yr(Y, rcol=rcol, wo=wo):
            for j in range(NDT):
                bb.inc(Y.dma_start(out=dslice(xr, j, 0, wo, W),
                                   in_=io["x_t"][j * 128:(j + 1) * 128, rcol:rcol + wo],
                                   allow_slow_non_contiguous=(wo == 1)), "yd", 16)
        bb.Y(_yr)

        emit_ln_stats(bb, x3, W, 0, wo, sq, W, ps_stat, rows, onescol, onesrow,
                      ps_b1, ps_b2, "ln3")
        def _vw(V):
            bb.wait(V, "yd")
            bb.inc(V.nop(), "v")
        bb.V(_vw)
        emit_ln_apply_res(bb, x3, W, x3, W, xr, W, 0, wo, 0, 0, ps_b1, ps_b2, gq, W)

        def _yst(Y, wo=wo, ocol=ocol):
            bb.wait(Y, "v")
            for j in range(NDT):
                bb.inc(Y.dma_start(out=io["xs1_t"][j * 128:(j + 1) * 128, ocol:ocol + wo],
                                   in_=dslice(x3, j, 0, wo, W),
                                   allow_slow_non_contiguous=(wo == 1)), "st", 16)
        bb.Y(_yst)

        bb.barrier("vtsy")

    def _yf(Y):
        bb.wait(Y, "st")
    bb.Y(_yf)


def build_B1(n):
    nc = bass.Bass("TRN2", target_bir_lowering=False, detect_race_conditions=False)
    io = {}
    io["x_t"] = nc.dram_tensor("x_t", [1024, 3 + n], F32, kind="ExternalInput").ap()
    io["x2p_t"] = nc.dram_tensor("x2p_t", [1024, 1 + n], F32, kind="ExternalInput").ap()
    io["g_t"] = nc.dram_tensor("g_t", [1024, 1 + n], F32, kind="ExternalInput").ap()
    io["h_in"] = nc.dram_tensor("h_in", [128, 8], F32, kind="ExternalInput").ap()
    io["ff1T"] = nc.dram_tensor("ff1T", [1024, 4096], F32, kind="ExternalInput").ap()
    io["ff2T"] = nc.dram_tensor("ff2T", [4096, 1024], F32, kind="ExternalInput").ap()
    io["vecb"] = nc.dram_tensor("vecb", [128, 48], F32, kind="ExternalInput").ap()
    io["consts"] = nc.dram_tensor("consts", [128, 4], F32, kind="ExternalInput").ap()
    io["xs1_t"] = nc.dram_tensor("xs1_t", [1024, 1 + n], F32, kind="ExternalOutput").ap()
    with contextlib.ExitStack() as st:
        bb = BB(nc, st)
        emit_stageB1(bb, nc, st, io, n)
    return nc



def emit_stageB2(bb, nc, st, io, n):
    e = st.enter_context
    W = 256
    ZW = 257
    nch = n // W

    xs1c = e(nc.sbuf_tensor("C_xs1", [P, NDT * W], F32))
    z = e(nc.sbuf_tensor("C_z", [P, NDT * ZW], F32))
    sq = e(nc.sbuf_tensor("C_sq", [P, NDT * W], F32))
    kvsb = e(nc.sbuf_tensor("C_kv", [P, NDT * W], F32))
    xrbf = e(nc.sbuf_tensor("C_xr", [P, NDT * W], BF16))
    xkbf = e(nc.sbuf_tensor("C_xk", [P, NDT * W], BF16))
    k2bf = e(nc.sbuf_tensor("C_k2", [P, NDT * W], BF16))
    ln4bf = e(nc.sbuf_tensor("C_ln4", [P, NDT * W], BF16))
    qpc = e(nc.sbuf_tensor("C_qpc", [P, NDT * W], BF16))
    scr = e(nc.sbuf_tensor("C_scr", [P, W], F32))
    scr2 = e(nc.sbuf_tensor("C_scr2", [P, W], F32))
    qdn = e(nc.sbuf_tensor("C_qdn", [P, W], BF16))
    kdn = e(nc.sbuf_tensor("C_kdn", [P, W], BF16))
    combsb = e(nc.sbuf_tensor("C_comb", [P, W], F32))
    nsq = e(nc.sbuf_tensor("C_nsq", [2, W], F32))
    qptm = e(nc.sbuf_tensor("C_qptm", [P, 128], F32))
    kptm = e(nc.sbuf_tensor("C_kptm", [P, 128], BF16))
    vtm = e(nc.sbuf_tensor("C_vtm", [P, 128], BF16))
    negm = e(nc.sbuf_tensor("C_negm", [P, 4], F32))
    rows = e(nc.sbuf_tensor("C_rows", [1, 4 * W], F32))
    wr = e(nc.sbuf_tensor("C_wr", [P, NDT * 1024], BF16))
    wk = e(nc.sbuf_tensor("C_wk", [P, NDT * 1024], BF16))
    wv = e(nc.sbuf_tensor("C_wv", [P, NDT * 1024], BF16))
    wqkv = e(nc.sbuf_tensor("C_wqkv", [P, NDT * 3072], BF16))
    blkpm = e(nc.sbuf_tensor("C_blkpm", [P, 128], BF16))
    ident = e(nc.sbuf_tensor("C_ident", [P, 128], F32))
    halfones = e(nc.sbuf_tensor("C_half", [P, 2], F32))
    negsel = e(nc.sbuf_tensor("C_negsel", [2, 128], F32))
    vecb2 = e(nc.sbuf_tensor("C_vec", [P, 40], F32))
    qb8 = e(nc.sbuf_tensor("C_qb8", [P, 16], F32))
    maskz = e(nc.sbuf_tensor("C_maskz", [P, 1], F32))
    consts = e(nc.sbuf_tensor("C_consts", [P, 4], F32))
    onescol = e(nc.sbuf_tensor("C_ones", [P, 1], F32))
    onescbf = e(nc.sbuf_tensor("C_onesbf", [P, 1], BF16))
    zer64 = e(nc.sbuf_tensor("C_zer64", [P, 64], BF16))
    onesrow = e(nc.sbuf_tensor("C_onesr", [1, P], F32))
    ones32 = e(nc.sbuf_tensor("C_ones32", [1, P], F32))
    kvout = e(nc.sbuf_tensor("C_kvout", [64, 1040], F32))

    ps_stat = e(nc.psum_tensor("C_pstat", [33, 512], F32))
    ps_b1 = e(nc.psum_tensor("C_psb1", [P, 512], F32))
    ps_b2 = e(nc.psum_tensor("C_psb2", [P, 512], F32))
    ps_mm = e(nc.psum_tensor("C_psmm", [P, 512], F32))
    ps_misc = e(nc.psum_tensor("C_psmisc", [P, 512], F32))
    ps_kv = e(nc.psum_tensor("C_pskv", [64, 1024], F32))
    ps_ks = e(nc.psum_tensor("C_psks", [64, 16], F32))

    nc.const_aps.register(F32, -1.0, consts[:, 0:1])
    nc.const_aps.register(F32, 0.125, consts[:, 1:2])
    nc.const_aps.register(F32, 1e-6, consts[:, 2:3])

    C_MR, C_MK, C_QB = 0, 8, 16

    def _g(Gp):
        for j in range(NDT):
            bb.inc(Gp.dma_start(out=wr[:, j * 1024:(j + 1) * 1024],
                                in_=io["WrT"][j * 128:(j + 1) * 128, :]), "gd", 16)
        for j in range(NDT):
            bb.inc(Gp.dma_start(out=wk[:, j * 1024:(j + 1) * 1024],
                                in_=io["WkT"][j * 128:(j + 1) * 128, :]), "gd", 16)
        for j in range(NDT):
            bb.inc(Gp.dma_start(out=wv[:, j * 1024:(j + 1) * 1024],
                                in_=io["WvT"][j * 128:(j + 1) * 128, :]), "gd", 16)
        for j in range(NDT):
            bb.inc(Gp.dma_start(out=wqkv[:, j * 3072:(j + 1) * 3072],
                                in_=io["qkvT"][j * 128:(j + 1) * 128, :]), "gd", 16)
        bb.inc(Gp.dma_start(out=blkpm[:], in_=io["blkpm"][:]), "gd", 16)
    bb.G(_g)

    def _y(Y):
        for nm, t in [("vecb2", vecb2), ("maskz", maskz), ("consts", consts),
                      ("ident", ident), ("halfones", halfones), ("negsel", negsel)]:
            bb.inc(Y.dma_start(out=t[:], in_=io[nm][:]), "yd", 16)
    bb.Y(_y)

    def _v(V):
        bb.wait(V, "yd")
        V.memset(onescol[:], 1.0)
        V.memset(onescbf[:], 1.0)
        V.memset(zer64[:], 0.0)
        V.memset(onesrow[:], 1.0)
        V.memset(ones32[:], 32.0)
        ins = V.tensor_scalar_mul(qb8[:], vecb2[:, C_QB:C_QB + 16], 0.125)
        bb.inc(ins, "v")
    bb.V(_v)

    def _tw(T):
        bb.wait(T, "gd")
        bb.wait(T, "yd")
        bb.wait(T, "v")
        T.matmul(ps_kv[0:64, 0:512], zer64[:], wqkv[:, 0:512],
                 start=True, stop=False, skip_group_check=True)
        T.matmul(ps_kv[0:64, 512:1024], zer64[:], wqkv[:, 0:512],
                 start=True, stop=False, skip_group_check=True)
        T.matmul(ps_ks[0:64, 0:16], zer64[:], wqkv[:, 0:16],
                 start=True, stop=False, skip_group_check=True)
    bb.T(_tw)
    bb.barrier("vtsy")

    def _ym(Y):
        for j in range(NDT):
            bb.inc(Y.dma_start(out=dslice(xs1c, j, 0, 1, W),
                               in_=io["xs1_t"][j * 128:(j + 1) * 128, 0:1],
                               allow_slow_non_contiguous=True), "yd", 16)
    bb.Y(_ym)
    def _vm(V):
        bb.wait(V, "yd")
        bb.inc(V.nop(), "v")
    bb.V(_vm)
    emit_ln_stats(bb, xs1c, W, 0, 1, sq, W, ps_stat, rows, onescol, onesrow,
                  ps_b1, ps_b2, "lnzb")
    emit_ln_apply(bb, xs1c, W, z, ZW, 0, 1, 0, ps_b1, ps_b2)
    def _vm2(V):
        for j in range(NDT):
            ins = V.tensor_scalar_mul(dslice(z, j, 0, 1, ZW), dslice(z, j, 0, 1, ZW),
                                      maskz[:, 0:1])
        bb.inc(ins, "v")
    bb.V(_vm2)
    bb.barrier("vtsy")

    for c in range(nch):
        if c > 0:
            def _vz(V):
                for j in range(NDT):
                    ins = V.tensor_copy(dslice(z, j, 0, 1, ZW), dslice(z, j, W, W + 1, ZW))
                bb.inc(ins, "v")
            bb.V(_vz)

        def _yl(Y, c=c):
            bb.wait(Y, "v")
            for j in range(NDT):
                bb.inc(Y.dma_start(out=dslice(xs1c, j, 0, W, W),
                                   in_=io["xs1_t"][j * 128:(j + 1) * 128, 1 + c * W:1 + (c + 1) * W]),
                       "yd", 16)
        bb.Y(_yl)
        def _vw(V):
            bb.wait(V, "yd")
            bb.inc(V.nop(), "v")
        bb.V(_vw)

        emit_ln_stats(bb, xs1c, W, 0, W, sq, W, ps_stat, rows, onescol, onesrow,
                      ps_b1, ps_b2, "lnz")
        emit_ln_apply(bb, xs1c, W, z, ZW, 0, W, 1, ps_b1, ps_b2)

        def _vmix(V):
            bb.wait(V, "v")
            for j in range(NDT):
                V.tensor_tensor(dslice(sq, j, 0, W, W), dslice(z, j, 1, ZW, ZW),
                                dslice(z, j, 0, W, ZW), ALU.subtract)
            for j in range(NDT):
                V.scalar_tensor_tensor(dslice(xrbf, j, 0, W, W), dslice(sq, j, 0, W, W),
                                       vecb2[:, C_MR + j:C_MR + j + 1],
                                       dslice(z, j, 0, W, ZW), ALU.mult, ALU.add)
            for j in range(NDT):
                ins = V.scalar_tensor_tensor(dslice(xkbf, j, 0, W, W), dslice(sq, j, 0, W, W),
                                             vecb2[:, C_MK + j:C_MK + j + 1],
                                             dslice(z, j, 0, W, ZW), ALU.mult, ALU.add)
            bb.inc(ins, "v")
        bb.V(_vmix)

        for et in range(NDT):
            emit_mm(bb, ps_mm, wk, 1024, xkbf, W, 0, W, et)
            def _vk(V, et=et):
                bb.wait(V, "t")
                ins = V.tensor_scalar_max(scr[:], ps_mm[:, 0:W], 0.0)
                bb.inc(ins, "v")
            bb.V(_vk)
            def _sk(S, et=et):
                bb.wait(S, "v")
                ins = S.activation(out=dslice(k2bf, et, 0, W, W), in_=scr[:], func=AF.Square)
                bb.inc(ins, "s")
            bb.S(_sk)

        for et in range(NDT):
            emit_mm(bb, ps_mm, wv, 1024, k2bf, W, 0, W, et)
            def _vv(V, et=et):
                bb.wait(V, "t")
                ins = V.tensor_copy(dslice(kvsb, et, 0, W, W), ps_mm[:, 0:W])
                bb.inc(ins, "v")
            bb.V(_vv)

        for et in range(NDT):
            emit_mm(bb, ps_mm, wr, 1024, xrbf, W, 0, W, et)
            def _sr(S, et=et):
                bb.wait(S, "t")
                ins = S.activation(out=scr[:], in_=ps_mm[:, 0:W], func=AF.Sigmoid)
                bb.inc(ins, "s")
            bb.S(_sr)
            def _vr(V, et=et):
                bb.wait(V, "s")
                ins = V.tensor_tensor(dslice(kvsb, et, 0, W, W), scr[:],
                                      dslice(kvsb, et, 0, W, W), ALU.mult)
                bb.inc(ins, "v")
            bb.V(_vr)

        emit_ln_stats(bb, kvsb, W, 0, W, sq, W, ps_stat, rows, onescol, onesrow,
                      ps_b1, ps_b2, "lno2")
        emit_ln_apply_res(bb, kvsb, W, xs1c, W, xs1c, W, 0, W, 0, 0, ps_b1, ps_b2, sq, W)

        def _ys2(Y, c=c):
            bb.wait(Y, "v")
            for j in range(NDT):
                bb.inc(Y.dma_start(out=io["xs2_t"][j * 128:(j + 1) * 128, c * W:(c + 1) * W],
                                   in_=dslice(xs1c, j, 0, W, W)), "st", 16)
        bb.Y(_ys2)

        emit_ln_stats(bb, xs1c, W, 0, W, sq, W, ps_stat, rows, onescol, onesrow,
                      ps_b1, ps_b2, "ln4")
        emit_ln_apply(bb, xs1c, W, ln4bf, W, 0, W, 0, ps_b1, ps_b2, mid=(sq, W))

        for p in range(NDT):
            emit_mm(bb, ps_mm, wqkv, 3072, ln4bf, W, 0, W, p)
            def _sq1(S, p=p):
                bb.wait(S, "t")
                S.activation(out=qdn[:], in_=ps_mm[:, 0:W], func=AF.Identity,
                             scale=0.125, bias=qb8[:, p:p + 1])
                ins = S.activation(out=scr[:], in_=ps_mm[:, 0:W], func=AF.Square,
                                   scale=0.125, bias=qb8[:, p:p + 1])
                bb.inc(ins, "s")
            bb.S(_sq1)
            def _tn(T, p=p):
                bb.wait(T, "s")
                ins = T.matmul(ps_misc[0:2, 0:W], halfones[:], scr[:], start=True, stop=True)
                bb.inc(ins, "t")
            bb.T(_tn)
            def _vn(V):
                bb.wait(V, "t")
                ins = V.tensor_copy(nsq[:], ps_misc[0:2, 0:W])
                bb.inc(ins, "v")
            bb.V(_vn)
            def _tc(T, p=p):
                bb.wait(T, "v")
                T.matmul(ps_mm[:, 0:W], blkpm[:], qdn[:], start=True, stop=False)
                ins = T.matmul(ps_mm[:, 0:W], negsel[:], nsq[:], start=False, stop=True)
                bb.inc(ins, "t")
            bb.T(_tc)
            def _vc(V):
                bb.wait(V, "t")
                ins = V.tensor_copy(combsb[:], ps_mm[:, 0:W])
                bb.inc(ins, "v")
            bb.V(_vc)
            for B in range(W // 128):
                def _tt(T, B=B):
                    bb.wait(T, "v")
                    ins = T.transpose(ps_misc[:, 0:128], combsb[:, B * 128:(B + 1) * 128], ident[:])
                    bb.inc(ins, "t")
                bb.T(_tt)
                def _vm3(V, B=B):
                    bb.wait(V, "t")
                    V.reduce_max(negm[:, 0:1], ps_misc[:, 0:64], axis=AX, negate=True)
                    ins = V.reduce_max(negm[:, 1:2], ps_misc[:, 64:128], axis=AX, negate=True)
                    bb.inc(ins, "v")
                bb.V(_vm3)
                def _se(S, B=B):
                    bb.wait(S, "v")
                    S.activation(out=qptm[:, 0:64], in_=ps_misc[:, 0:64], func=AF.Exp,
                                 bias=negm[:, 0:1])
                    ins = S.activation(out=qptm[:, 64:128], in_=ps_misc[:, 64:128], func=AF.Exp,
                                       bias=negm[:, 1:2])
                    bb.inc(ins, "s")
                bb.S(_se)
                def _vq(V, B=B):
                    bb.wait(V, "s")
                    ins = V.tensor_scalar(qptm[:], qptm[:], 0.125, 1e-6, ALU.mult, ALU.add)
                    bb.inc(ins, "v")
                bb.V(_vq)
                def _tq(T, B=B):
                    bb.wait(T, "v")
                    ins = T.transpose(ps_misc[:, 128:256], qptm[:], ident[:])
                    bb.inc(ins, "t")
                bb.T(_tq)
                def _vq2(V, p=p, B=B):
                    bb.wait(V, "t")
                    ins = V.tensor_copy(dslice(qpc, p, B * 128, (B + 1) * 128, W),
                                        ps_misc[:, 128:256])
                    bb.inc(ins, "v")
                bb.V(_vq2)
            emit_mm(bb, ps_mm, wqkv, 3072, ln4bf, W, 0, W, 16 + p)
            def _sv(S, p=p):
                bb.wait(S, "t")
                ins = S.activation(out=scr2[:], in_=ps_mm[:, 0:W], func=AF.Identity,
                                   bias=vecb2[:, C_QB + 16 + p:C_QB + 16 + p + 1])
                bb.inc(ins, "s")
            bb.S(_sv)
            emit_mm(bb, ps_mm, wqkv, 3072, ln4bf, W, 0, W, 8 + p, wait=("v", "s"))
            def _sk2(S, p=p):
                bb.wait(S, "t")
                S.activation(out=kdn[:], in_=ps_mm[:, 0:W], func=AF.Identity,
                             scale=0.125, bias=qb8[:, 8 + p:8 + p + 1])
                ins = S.activation(out=scr[:], in_=ps_mm[:, 0:W], func=AF.Square,
                                   scale=0.125, bias=qb8[:, 8 + p:8 + p + 1])
                bb.inc(ins, "s")
            bb.S(_sk2)
            def _tn2(T, p=p):
                bb.wait(T, "s")
                ins = T.matmul(ps_misc[0:2, 0:W], halfones[:], scr[:], start=True, stop=True)
                bb.inc(ins, "t")
            bb.T(_tn2)
            def _vn2(V):
                bb.wait(V, "t")
                ins = V.tensor_copy(nsq[:], ps_misc[0:2, 0:W])
                bb.inc(ins, "v")
            bb.V(_vn2)
            def _tc2(T, p=p):
                bb.wait(T, "v")
                T.matmul(ps_mm[:, 0:W], blkpm[:], kdn[:], start=True, stop=False)
                ins = T.matmul(ps_mm[:, 0:W], negsel[:], nsq[:], start=False, stop=True)
                bb.inc(ins, "t")
            bb.T(_tc2)
            def _vc2(V):
                bb.wait(V, "t")
                ins = V.tensor_copy(combsb[:], ps_mm[:, 0:W])
                bb.inc(ins, "v")
            bb.V(_vc2)
            for B in range(W // 128):
                def _tt2(T, B=B):
                    bb.wait(T, "v")
                    T.transpose(ps_misc[:, 0:128], combsb[:, B * 128:(B + 1) * 128], ident[:])
                    ins = T.transpose(ps_misc[:, 256:384], scr2[:, B * 128:(B + 1) * 128], ident[:])
                    bb.inc(ins, "t")
                bb.T(_tt2)
                def _vm4(V, B=B):
                    bb.wait(V, "t")
                    V.reduce_max(negm[:, 2:3], ps_misc[:, 0:64], axis=AX, negate=True)
                    V.reduce_max(negm[:, 3:4], ps_misc[:, 64:128], axis=AX, negate=True)
                    ins = V.tensor_copy(vtm[:], ps_misc[:, 256:384])
                    bb.inc(ins, "v")
                bb.V(_vm4)
                def _se2(S, B=B):
                    bb.wait(S, "v")
                    S.activation(out=qptm[:, 0:64], in_=ps_misc[:, 0:64], func=AF.Exp,
                                 bias=negm[:, 2:3])
                    ins = S.activation(out=qptm[:, 64:128], in_=ps_misc[:, 64:128], func=AF.Exp,
                                       bias=negm[:, 3:4])
                    bb.inc(ins, "s")
                bb.S(_se2)
                def _vk2(V, B=B):
                    bb.wait(V, "s")
                    ins = V.tensor_scalar(kptm[:], qptm[:], 0.125, 1e-6, ALU.mult, ALU.add)
                    bb.inc(ins, "v")
                bb.V(_vk2)
                def _tkv(T, p=p, B=B, c=c):
                    bb.wait(T, "v")
                    for s_ in range(2):
                        h = 2 * p + s_
                        T.matmul(ps_kv[0:64, h * 64:(h + 1) * 64],
                                 kptm[:, s_ * 64:(s_ + 1) * 64], vtm[:, s_ * 64:(s_ + 1) * 64],
                                 start=False, stop=False, skip_group_check=True)
                        ins = T.matmul(ps_ks[0:64, h:h + 1],
                                       kptm[:, s_ * 64:(s_ + 1) * 64], onescbf[:],
                                       start=False, stop=False, skip_group_check=True)
                    bb.inc(ins, "t")
                bb.T(_tkv)
        def _yqp(Y, c=c):
            bb.wait(Y, "v")
            for j in range(NDT):
                bb.inc(Y.dma_start(out=io["qp_t"][j * 128:(j + 1) * 128, c * W:(c + 1) * W],
                                   in_=dslice(qpc, j, 0, W, W)), "st", 16)
        bb.Y(_yqp)
        bb.barrier("vtsy")

    def _vkv(V):
        bb.wait(V, "t")
        for h in range(16):
            V.tensor_copy(kvout[0:64, 65 * h:65 * h + 64], ps_kv[0:64, h * 64:(h + 1) * 64])
        ins = V.tensor_copy(kvout[0:64, 64::65], ps_ks[0:64, 0:16])
        bb.inc(ins, "v")
    bb.V(_vkv)
    def _ykv(Y):
        bb.wait(Y, "v")
        bb.inc(Y.dma_start(out=io["kvp"][:], in_=kvout[:]), "st", 16)
        bb.wait(Y, "st")
    bb.Y(_ykv)


def build_B2(n):
    nc = bass.Bass("TRN2", target_bir_lowering=False, detect_race_conditions=False)
    io = {}
    io["xs1_t"] = nc.dram_tensor("xs1_t", [1024, 1 + n], F32, kind="ExternalInput").ap()
    io["WrT"] = nc.dram_tensor("WrT", [1024, 1024], F32, kind="ExternalInput").ap()
    io["WkT"] = nc.dram_tensor("WkT", [1024, 1024], F32, kind="ExternalInput").ap()
    io["WvT"] = nc.dram_tensor("WvT", [1024, 1024], F32, kind="ExternalInput").ap()
    io["qkvT"] = nc.dram_tensor("qkvT", [1024, 3072], F32, kind="ExternalInput").ap()
    io["blkpm"] = nc.dram_tensor("blkpm", [128, 128], F32, kind="ExternalInput").ap()
    io["vecb2"] = nc.dram_tensor("vecb2", [128, 40], F32, kind="ExternalInput").ap()
    io["maskz"] = nc.dram_tensor("maskz", [128, 1], F32, kind="ExternalInput").ap()
    io["consts"] = nc.dram_tensor("consts", [128, 4], F32, kind="ExternalInput").ap()
    io["ident"] = nc.dram_tensor("ident", [128, 128], F32, kind="ExternalInput").ap()
    io["halfones"] = nc.dram_tensor("halfones", [128, 2], F32, kind="ExternalInput").ap()
    io["negsel"] = nc.dram_tensor("negsel", [2, 128], F32, kind="ExternalInput").ap()
    io["xs2_t"] = nc.dram_tensor("xs2_t", [1024, n], F32, kind="ExternalOutput").ap()
    io["qp_t"] = nc.dram_tensor("qp_t", [1024, n], BF16, kind="ExternalOutput").ap()
    io["kvp"] = nc.dram_tensor("kvp", [64, 1040], F32, kind="ExternalOutput").ap()
    with contextlib.ExitStack() as st:
        bb = BB(nc, st)
        emit_stageB2(bb, nc, st, io, n)
    return nc



def emit_stageC(bb, nc, st, io, n, dbg_xs3=False):
    e = st.enter_context
    consts = e(nc.sbuf_tensor("D_consts", [P, 4], F32))
    onescol = e(nc.sbuf_tensor("D_ones", [P, 1], F32))
    onesrow = e(nc.sbuf_tensor("D_onesr", [1, P], F32))
    ones32 = e(nc.sbuf_tensor("D_ones32", [1, P], F32))
    ident = e(nc.sbuf_tensor("D_ident", [P, 128], F32))
    rows = e(nc.sbuf_tensor("D_rows", [1, 4 * 512], F32))
    vecc = e(nc.sbuf_tensor("D_vecc", [P, 8], F32))
    bgrow = e(nc.sbuf_tensor("D_bgrow", [1, 8], F32))
    wgsb = e(nc.sbuf_tensor("D_wg", [P, NDT * 8], F32))
    sel8 = e(nc.sbuf_tensor("D_sel8", [8, 1024], F32))
    PS = [e(nc.psum_tensor(f"D_p{i}", [P, 512], F32)) for i in range(8)]
    ps_stat, ps_b1, ps_b2, ps_mm, ps_misc, ps_u, ps_v, ps_sp = PS

    nc.const_aps.register(F32, -1.0, consts[:, 0:1])
    nc.const_aps.register(F32, 0.125, consts[:, 1:2])
    nc.const_aps.register(F32, 1e-6, consts[:, 2:3])

    xs3_t = nc.dram_tensor("xs3_t", [1024, n], F32,
                           kind=("ExternalOutput" if dbg_xs3 else "Internal")).ap()
    if dbg_xs3:
        dbg_rows = nc.dram_tensor("dbg_rows", [8, 1024], F32, kind="ExternalOutput").ap()
        dbg_oacc = nc.dram_tensor("dbg_oacc", [128, 8 * 1024], F32, kind="ExternalOutput").ap()
        dbg_lg = nc.dram_tensor("dbg_lg", [128, 16], F32, kind="ExternalOutput").ap()

    def _y0(Y):
        for nm, t in [("consts", consts), ("ident", ident), ("vecc", vecc),
                      ("bgrow", bgrow), ("Wg", wgsb), ("sel8", sel8)]:
            bb.inc(Y.dma_start(out=t[:], in_=io[nm][:]), "yd", 16)
    bb.Y(_y0)

    def _v0(V):
        bb.wait(V, "yd")
        V.memset(onescol[:], 1.0)
        V.memset(onesrow[:], 1.0)
        ins = V.memset(ones32[:], 32.0)
        bb.inc(ins, "v")
    bb.V(_v0)

    with contextlib.ExitStack() as st1:
        e1 = st1.enter_context
        W = 512
        qpc = e1(nc.sbuf_tensor("D_qpc", [P, NDT * W], BF16))
        xs2c = e1(nc.sbuf_tensor("D_xs2c", [P, NDT * W], F32))
        prbf = e1(nc.sbuf_tensor("D_prbf", [P, NDT * W], BF16))
        o3 = e1(nc.sbuf_tensor("D_o3", [P, NDT * W], F32))
        sq1 = e1(nc.sbuf_tensor("D_sq1", [P, NDT * W], F32))
        dens = e1(nc.sbuf_tensor("D_dens", [65, W], F32))
        ones64 = e1(nc.sbuf_tensor("D_ones64", [65, 128], F32))
        bcs = e1(nc.sbuf_tensor("D_bcs", [64, W], F32))
        kvsum = e1(nc.sbuf_tensor("D_kvsum", [64, 1040], F32))
        kvbf = e1(nc.sbuf_tensor("D_kvbf", [128, 1040], BF16))
        wproj = e1(nc.sbuf_tensor("D_wproj", [P, NDT * 1024], BF16))

        def _g1(Gp):
            for j in range(NDT):
                bb.inc(Gp.dma_start(out=wproj[:, j * 1024:(j + 1) * 1024],
                                    in_=io["projT"][j * 128:(j + 1) * 128, :]), "gd", 16)
        bb.G(_g1)
        def _y1(Y):
            bb.inc(Y.dma_start(out=kvsum[:], in_=io["kvp_a"][:]), "yd", 16)
        bb.Y(_y1)
        kvtmp = e1(nc.sbuf_tensor("D_kvtmp", [64, 1040], F32))
        def _y2(Y):
            bb.inc(Y.dma_start(out=kvtmp[:], in_=io["kvp_b"][:]), "yd", 16)
        bb.Y(_y2)
        def _v1(V):
            bb.wait(V, "yd")
            V.memset(ones64[64:65, :], 1.0)
            V.tensor_tensor(kvsum[:], kvsum[:], kvtmp[:], ALU.add)
            ins = V.tensor_copy(kvbf[0:64, :], kvsum[:])
            bb.inc(ins, "v")
        bb.V(_v1)
        def _g2(Gp):
            bb.wait(Gp, "v")
            bb.inc(Gp.dma_start(out=kvbf[64:128, :], in_=kvbf[0:64, :]), "gd", 16)
        bb.G(_g2)
        def _tw(T):
            bb.wait(T, "gd")
            bb.wait(T, "yd")
        bb.T(_tw)
        bb.barrier("vtsy")

        for c in range(n // W):
            def _yl(Y, c=c):
                bb.wait(Y, "v")
                for j in range(NDT):
                    bb.inc(Y.dma_start(out=dslice(qpc, j, 0, W, W),
                                       in_=io["qp_t"][j * 128:(j + 1) * 128, c * W:(c + 1) * W]),
                           "yd", 16)
                for j in range(NDT):
                    bb.inc(Y.dma_start(out=dslice(xs2c, j, 0, W, W),
                                       in_=io["xs2_t"][j * 128:(j + 1) * 128, c * W:(c + 1) * W]),
                           "yd", 16)
            bb.Y(_yl)
            def _vw(V):
                bb.wait(V, "yd")
                bb.inc(V.nop(), "v")
            bb.V(_vw)

            for h in range(16):
                def _to(T, h=h):
                    bb.wait(T, "v")
                    hb = 64 * (h % 2)
                    ins = T.matmul(ps_mm[0:65, 0:W], kvbf[hb:hb + 64, 65 * h:65 * h + 65],
                                   qpc[hb:hb + 64, (h // 2) * W:(h // 2 + 1) * W],
                                   start=True, stop=True)
                    bb.inc(ins, "t")
                bb.T(_to)
                def _vd(V, h=h):
                    bb.wait(V, "t")
                    bb.inc(V.tensor_scalar_add(dens[64:65, :], ps_mm[64:65, 0:W], 1e-6), "v")
                    bb.wait(V, "v")
                    ins = V.reciprocal(dens[64:65, :], dens[64:65, :])
                    bb.inc(ins, "v")
                bb.V(_vd)
                def _tb(T, h=h):
                    bb.wait(T, "v")
                    ins = T.matmul(ps_misc[0:64, 0:W], ones64[64:65, 0:64], dens[64:65, :],
                                   start=True, stop=True)
                    bb.inc(ins, "t")
                bb.T(_tb)
                def _vb(V, h=h):
                    bb.wait(V, "t")
                    V.tensor_copy(bcs[:], ps_misc[0:64, 0:W])
                    ins = V.tensor_tensor(
                        prbf[64 * (h % 2):64 * (h % 2) + 64, (h // 2) * W:(h // 2 + 1) * W],
                        ps_mm[0:64, 0:W], bcs[:], ALU.mult)
                    bb.inc(ins, "v")
                bb.V(_vb)

            for et in range(NDT):
                emit_mm(bb, ps_mm, wproj, 1024, prbf, W, 0, W, et)
                def _so(S, et=et):
                    bb.wait(S, "t")
                    ins = S.activation(out=dslice(o3, et, 0, W, W), in_=ps_mm[:, 0:W],
                                       func=AF.Identity, bias=vecc[:, et:et + 1])
                    bb.inc(ins, "s")
                bb.S(_so)

            emit_ln_stats(bb, o3, W, 0, W, sq1, W, ps_stat, rows, onescol, onesrow,
                          ps_b1, ps_b2, "lno3")
            emit_ln_apply_res(bb, o3, W, xs2c, W, xs2c, W, 0, W, 0, 0, ps_b1, ps_b2, sq1, W)

            def _ys(Y, c=c):
                bb.wait(Y, "v")
                for j in range(NDT):
                    bb.inc(Y.dma_start(out=xs3_t[j * 128:(j + 1) * 128, c * W:(c + 1) * W],
                                       in_=dslice(xs2c, j, 0, W, W)), "st", 16)
            bb.Y(_ys)
            bb.barrier("vtsy")

    def _ywait(Y):
        bb.wait(Y, "st")
    bb.Y(_ywait)
    with contextlib.ExitStack() as st2:
        e2 = st2.enter_context
        SP = min(1024, n)
        SW = 512
        NSUB = SP // SW
        ln3bf = e2(nc.sbuf_tensor("D_ln3", [P, NDT * SP], BF16))
        outacc = e2(nc.sbuf_tensor("D_oacc", [P, NDT * SP], F32))
        h1bf = e2(nc.sbuf_tensor("D_h1", [P, 32 * SP], BF16))
        x3c = e2(nc.sbuf_tensor("D_x3c", [P, NDT * SW], F32))
        sqc = e2(nc.sbuf_tensor("D_sqc", [P, NDT * SW], F32))
        lnf = e2(nc.sbuf_tensor("D_lnf", [P, NDT * SW], F32))
        scrA = e2(nc.sbuf_tensor("D_scrA", [P, SW], F32))
        scrB = e2(nc.sbuf_tensor("D_scrB", [P, SW], F32))
        scbsb = e2(nc.sbuf_tensor("D_scb", [P, SP], F32))
        scrows = e2(nc.sbuf_tensor("D_scrows", [8, SP], F32))
        lgsb = e2(nc.sbuf_tensor("D_lg", [P, 8], F32))
        mx8 = e2(nc.sbuf_tensor("D_mx8", [P, 8], F32))
        m1t = e2(nc.sbuf_tensor("D_m1t", [P, 8], F32))
        m2t = e2(nc.sbuf_tensor("D_m2t", [P, 8], F32))
        sc8 = e2(nc.sbuf_tensor("D_sc8", [P, 8], F32))
        wcols = e2(nc.sbuf_tensor("D_wc", [P, 3], F32))
        w1sl = e2(nc.sbuf_tensor("D_w1sl", [P, 1024], BF16))
        w3sl = e2(nc.sbuf_tensor("D_w3sl", [P, 1024], BF16))
        w2sl = e2(nc.sbuf_tensor("D_w2sl", [P, 1024], BF16))

        nsp = n // SP
        for spi in range(nsp):
            base = spi * SP
            for sub in range(NSUB):
                sb = base + sub * SW
                def _yl(Y, sb=sb):
                    bb.wait(Y, "v")
                    for j in range(NDT):
                        bb.inc(Y.dma_start(out=dslice(x3c, j, 0, SW, SW),
                                           in_=xs3_t[j * 128:(j + 1) * 128, sb:sb + SW]),
                               "yd", 16)
                bb.Y(_yl)
                def _vw(V):
                    bb.wait(V, "yd")
                    bb.inc(V.nop(), "v")
                bb.V(_vw)
                emit_ln_stats(bb, x3c, SW, 0, SW, sqc, SW, ps_stat, rows, onescol,
                              onesrow, ps_b1, ps_b2, "ln3m")
                emit_ln_apply(bb, x3c, SW, lnf, SW, 0, SW, 0, ps_b1, ps_b2)
                def _vc(V, sub=sub):
                    bb.wait(V, "v")
                    for j in range(NDT):
                        ins = V.tensor_copy(dslice(ln3bf, j, sub * SW, (sub + 1) * SW, SP),
                                            dslice(lnf, j, 0, SW, SW))
                    bb.inc(ins, "v")
                bb.V(_vc)
                for tt in range(SW // 128):
                    def _tl(T, tt=tt):
                        bb.wait(T, "v")
                        for j in range(NDT):
                            T.matmul(ps_misc[0:128, 0:8],
                                     dslice(lnf, j, tt * 128, (tt + 1) * 128, SW),
                                     wgsb[:, j * 8:(j + 1) * 8],
                                     start=(j == 0), stop=False)
                        ins = T.matmul(ps_misc[0:128, 0:8], onesrow[0:1, :], bgrow[:],
                                       start=False, stop=True)
                        bb.inc(ins, "t")
                    bb.T(_tl)
                    def _vr(V, tt=tt):
                        bb.wait(V, "t")
                        bb.inc(V.tensor_copy(lgsb[:], ps_misc[0:128, 0:8]), "v")
                        bb.wait(V, "v")
                        ins = V.max(mx8[:], lgsb[:])
                        bb.inc(ins, "v")
                    bb.V(_vr)
                    def _vr2(V):
                        bb.wait(V, "v")
                        ins = V.tensor_tensor(wcols[:, 2:3], mx8[:, 0:1], mx8[:, 1:2],
                                              ALU.subtract)
                        bb.inc(ins, "v")
                    bb.V(_vr2)
                    if dbg_xs3 and spi == 0 and sub == 0 and tt == 0:
                        def _ydl(Y):
                            bb.wait(Y, "v")
                            bb.inc(Y.dma_start(out=dbg_lg[:, 0:8], in_=lgsb[:]), "st", 16)
                            bb.inc(Y.dma_start(out=dbg_lg[:, 8:16], in_=mx8[:]), "st", 16)
                        bb.Y(_ydl)
                    def _sw(S):
                        bb.wait(S, "v")
                        S.activation(out=wcols[:, 0:1], in_=wcols[:, 2:3], func=AF.Sigmoid)
                        ins = S.activation(out=wcols[:, 1:2], in_=wcols[:, 2:3],
                                           func=AF.Sigmoid, scale=-1.0)
                        bb.inc(ins, "s")
                    bb.S(_sw)
                    def _vm(V, tt=tt, sub=sub):
                        bb.wait(V, "s")
                        V.tensor_scalar(m1t[:], lgsb[:], mx8[:, 0:1], None, ALU.is_equal)
                        V.tensor_scalar(m2t[:], lgsb[:], mx8[:, 1:2], None, ALU.is_equal)
                        V.tensor_scalar(sc8[:], m1t[:], wcols[:, 0:1], None, ALU.mult)
                        ins = V.scalar_tensor_tensor(sc8[:], m2t[:], wcols[:, 1:2], sc8[:],
                                                     ALU.mult, ALU.add)
                        bb.inc(ins, "v")
                    bb.V(_vm)
                    def _tt2(T, tt=tt):
                        bb.wait(T, "v")
                        ins = T.transpose(ps_sp[0:8, 0:128], sc8[:], ident[:])
                        bb.inc(ins, "t")
                    bb.T(_tt2)
                    def _vt(V, tt=tt, sub=sub):
                        bb.wait(V, "t")
                        ti = sub * (SW // 128) + tt
                        ins = V.tensor_copy(scrows[0:8, ti * 128:(ti + 1) * 128],
                                            ps_sp[0:8, 0:128])
                        bb.inc(ins, "v")
                    bb.V(_vt)

            if dbg_xs3 and spi == 0:
                def _ydr(Y):
                    bb.wait(Y, "v")
                    bb.inc(Y.dma_start(out=dbg_rows[0:8, 0:SP], in_=scrows[:]), "st", 16)
                bb.Y(_ydr)
            for ex in range(8):
                for sub in range(NSUB):
                    def _ts(T, ex=ex, sub=sub):
                        bb.wait(T, "v")
                        ins = T.matmul(ps_sp[0:128, 0:SW], sel8[0:8, ex * 128:(ex + 1) * 128],
                                       scrows[0:8, sub * SW:(sub + 1) * SW],
                                       start=True, stop=True)
                        bb.inc(ins, "t")
                    bb.T(_ts)
                    def _vs(V, sub=sub):
                        bb.wait(V, "t")
                        ins = V.tensor_copy(scbsb[:, sub * SW:(sub + 1) * SW],
                                            ps_sp[0:128, 0:SW])
                        bb.inc(ins, "v")
                    bb.V(_vs)
                for ht in range(32):
                    def _gw(Gp, ex=ex, ht=ht):
                        bb.wait(Gp, "t")
                        bb.inc(Gp.dma_start(out=w1sl[:], in_=io["W1h"][ex, ht]), "wd", 16)
                        bb.inc(Gp.dma_start(out=w3sl[:], in_=io["W3h"][ex, ht]), "wd", 16)
                    bb.G(_gw)
                    for s2 in range(NSUB):
                        def _tu(T, ht=ht, s2=s2):
                            bb.wait(T, "wd")
                            bb.wait(T, "v")
                            for kt in range(NDT):
                                T.matmul(ps_u[:, 0:SW], w1sl[:, kt * 128:(kt + 1) * 128],
                                         dslice(ln3bf, kt, s2 * SW, (s2 + 1) * SW, SP),
                                         start=(kt == 0), stop=(kt == NDT - 1))
                            for kt in range(NDT):
                                ins = T.matmul(ps_v[:, 0:SW], w3sl[:, kt * 128:(kt + 1) * 128],
                                               dslice(ln3bf, kt, s2 * SW, (s2 + 1) * SW, SP),
                                               start=(kt == 0), stop=(kt == NDT - 1))
                            bb.inc(ins, "t")
                        bb.T(_tu)
                        def _ve(V, ht=ht, s2=s2):
                            bb.wait(V, "t")
                            ins = V.tensor_tensor(scrA[:], ps_u[:, 0:SW],
                                                  scbsb[:, s2 * SW:(s2 + 1) * SW], ALU.mult)
                            bb.inc(ins, "v")
                        bb.V(_ve)
                        def _ss(S, ht=ht, s2=s2):
                            bb.wait(S, "v")
                            ins = S.activation(out=scrB[:], in_=scrA[:], func=AF.Silu)
                            bb.inc(ins, "s")
                        bb.S(_ss)
                        def _ve2(V, ht=ht, s2=s2):
                            bb.wait(V, "s")
                            V.tensor_tensor(scrA[:], scrB[:],
                                            scbsb[:, s2 * SW:(s2 + 1) * SW], ALU.mult)
                            ins = V.tensor_tensor(
                                h1bf[:, ht * SP + s2 * SW: ht * SP + (s2 + 1) * SW],
                                scrA[:], ps_v[:, 0:SW], ALU.mult)
                            bb.inc(ins, "v")
                        bb.V(_ve2)
                for s2 in range(NSUB):
                    for kt in range(32):
                        def _gw2(Gp, ex=ex, kt=kt):
                            bb.wait(Gp, "t")
                            bb.inc(Gp.dma_start(
                                out=w2sl[:],
                                in_=io["W2"][ex * 4096 + kt * 128: ex * 4096 + (kt + 1) * 128, :]),
                                "wd", 16)
                        bb.G(_gw2)
                        def _ty(T, kt=kt, s2=s2):
                            bb.wait(T, "wd")
                            if kt == 0:
                                bb.wait(T, "v")
                            for dt in range(NDT):
                                ins = T.matmul(PS[dt][:, 0:SW],
                                               w2sl[:, dt * 128:(dt + 1) * 128],
                                               h1bf[:, kt * SP + s2 * SW: kt * SP + (s2 + 1) * SW],
                                               start=(kt == 0), stop=(kt == 31))
                            bb.inc(ins, "t")
                        bb.T(_ty)
                    def _vy(V, ex=ex, s2=s2):
                        bb.wait(V, "t")
                        for dt in range(NDT):
                            col = s2 * SW
                            if ex == 0:
                                ins = V.tensor_copy(dslice(outacc, dt, col, col + SW, SP),
                                                    PS[dt][:, 0:SW])
                            else:
                                ins = V.tensor_tensor(dslice(outacc, dt, col, col + SW, SP),
                                                      dslice(outacc, dt, col, col + SW, SP),
                                                      PS[dt][:, 0:SW], ALU.add)
                        bb.inc(ins, "v")
                    bb.V(_vy)

            if dbg_xs3 and spi == 0:
                def _ydo(Y):
                    bb.wait(Y, "v")
                    bb.inc(Y.dma_start(out=dbg_oacc[:, 0:NDT * SP], in_=outacc[:]), "st", 16)
                bb.Y(_ydo)
            for sub in range(NSUB):
                sb = base + sub * SW
                def _yl2(Y, sb=sb):
                    bb.wait(Y, "v")
                    for j in range(NDT):
                        bb.inc(Y.dma_start(out=dslice(x3c, j, 0, SW, SW),
                                           in_=xs3_t[j * 128:(j + 1) * 128, sb:sb + SW]),
                               "yd", 16)
                bb.Y(_yl2)
                def _vw2(V):
                    bb.wait(V, "yd")
                    bb.inc(V.nop(), "v")
                bb.V(_vw2)
                emit_ln_stats(bb, x3c, SW, 0, SW, sqc, SW, ps_stat, rows, onescol,
                              onesrow, ps_b1, ps_b2, "ln3f")
                emit_ln_apply(bb, x3c, SW, lnf, SW, 0, SW, 0, ps_b1, ps_b2)
                def _vmo(V, sub=sub):
                    bb.wait(V, "v")
                    for j in range(NDT):
                        ins = V.tensor_tensor(dslice(lnf, j, 0, SW, SW),
                                              dslice(lnf, j, 0, SW, SW),
                                              dslice(outacc, j, sub * SW, (sub + 1) * SW, SP),
                                              ALU.add)
                    bb.inc(ins, "v")
                bb.V(_vmo)
                emit_ln_stats(bb, lnf, SW, 0, SW, sqc, SW, ps_stat, rows, onescol,
                              onesrow, ps_b1, ps_b2, "lnmo")
                emit_ln_apply_res(bb, lnf, SW, x3c, SW, x3c, SW, 0, SW, 0, 0,
                                  ps_b1, ps_b2, sqc, SW)
                def _yo(Y, sb=sb):
                    bb.wait(Y, "v")
                    for j in range(NDT):
                        bb.inc(Y.dma_start(out=io["xout_t"][j * 128:(j + 1) * 128, sb:sb + SW],
                                           in_=dslice(x3c, j, 0, SW, SW)), "st", 16)
                bb.Y(_yo)
            bb.barrier("vtsyg")

    def _yf(Y):
        bb.wait(Y, "st")
    bb.Y(_yf)


def build_C(n, dbg_xs3=False):
    nc = bass.Bass("TRN2", target_bir_lowering=False, detect_race_conditions=False)
    io = {}
    io["xs2_t"] = nc.dram_tensor("xs2_t", [1024, n], F32, kind="ExternalInput").ap()
    io["qp_t"] = nc.dram_tensor("qp_t", [1024, n], BF16, kind="ExternalInput").ap()
    io["kvp_a"] = nc.dram_tensor("kvp_a", [64, 1040], F32, kind="ExternalInput").ap()
    io["kvp_b"] = nc.dram_tensor("kvp_b", [64, 1040], F32, kind="ExternalInput").ap()
    io["projT"] = nc.dram_tensor("projT", [1024, 1024], F32, kind="ExternalInput").ap()
    io["Wg"] = nc.dram_tensor("Wg", [128, 64], F32, kind="ExternalInput").ap()
    io["bgrow"] = nc.dram_tensor("bgrow", [1, 8], F32, kind="ExternalInput").ap()
    io["sel8"] = nc.dram_tensor("sel8", [8, 1024], F32, kind="ExternalInput").ap()
    io["W1h"] = nc.dram_tensor("W1h", [8, 32, 128, 1024], F32, kind="ExternalInput").ap()
    io["W3h"] = nc.dram_tensor("W3h", [8, 32, 128, 1024], F32, kind="ExternalInput").ap()
    io["W2"] = nc.dram_tensor("W2", [8 * 4096, 1024], F32, kind="ExternalInput").ap()
    io["vecc"] = nc.dram_tensor("vecc", [128, 8], F32, kind="ExternalInput").ap()
    io["ident"] = nc.dram_tensor("ident", [128, 128], F32, kind="ExternalInput").ap()
    io["consts"] = nc.dram_tensor("consts", [128, 4], F32, kind="ExternalInput").ap()
    io["xout_t"] = nc.dram_tensor("xout_t", [1024, n], F32, kind="ExternalOutput").ap()
    with contextlib.ExitStack() as st:
        bb = BB(nc, st)
        emit_stageC(bb, nc, st, io, n, dbg_xs3=dbg_xs3)
    return nc


from concourse.bass_utils import run_bass_kernel_spmd

_N = 4096


def _pk(v, k=8):
    return np.ascontiguousarray(np.asarray(v, np.float32).reshape(k, 128).T)


def kernel(L0_attn_x, L0_attn_conv_dw_w, L0_attn_conv_dw_b, L0_attn_conv_pw_w,
           L0_attn_conv_pw_b, L0_attn_conv_norm_g, L0_attn_gru_norm_g, L0_attn_gru_w,
           L0_attn_ff_norm_g, L0_attn_ff_w1, L0_attn_ff_b1, L0_attn_ff_w2, L0_attn_ff_b2,
           L0_ffn_W_r, L0_ffn_W_k, L0_ffn_W_v, L0_ffn_mu_r, L0_ffn_mu_k,
           L1_attn_qkv_w, L1_attn_qkv_b, L1_attn_proj_w, L1_attn_proj_b,
           L1_attn_projection_matrix, L1_ffn_Wg, L1_ffn_bg, L1_ffn_W1, L1_ffn_W2,
           L1_ffn_W3):
    x = np.asarray(L0_attn_x, np.float32)
    B, T, D = x.shape
    n = _N
    cores = list(range(8))

    x_ts = []
    for c in cores:
        b, hf = c // 2, c % 2
        if hf == 0:
            xa = np.concatenate([np.zeros((3, D), np.float32), x[b, 0:n]], 0)
        else:
            xa = x[b, n - 3:2 * n]
        x_ts.append(np.ascontiguousarray(xa.T))

    consts = np.stack([np.full(128, -1.0), np.full(128, 0.125), np.full(128, 1e-6),
                       np.zeros(128)], 1).astype(np.float32)

    pwT = np.ascontiguousarray(np.asarray(L0_attn_conv_pw_w, np.float32).T)
    gruT = np.ascontiguousarray(np.asarray(L0_attn_gru_w, np.float32).T)
    wdw = np.asarray(L0_attn_conv_dw_w, np.float32)
    veca = np.concatenate([
        _pk(L0_attn_conv_norm_g), _pk(L0_attn_gru_norm_g), _pk(L0_attn_conv_dw_b),
        _pk(wdw[:, 0, 0]), _pk(wdw[:, 0, 1]), _pk(wdw[:, 0, 2]),
        _pk(L0_attn_conv_pw_b)], axis=1).astype(np.float32)
    ncA = build_A(n)
    imA = [{"x_t": x_ts[c], "pwT": pwT, "gruT": gruT, "veca": veca, "consts": consts}
           for c in cores]
    rA = run_bass_kernel_spmd(ncA, imA, core_ids=cores).results

    ff1T = np.ascontiguousarray(np.asarray(L0_attn_ff_w1, np.float32).T)
    ff2T = np.ascontiguousarray(np.asarray(L0_attn_ff_w2, np.float32).T)
    vecb = np.concatenate([_pk(L0_attn_ff_norm_g), _pk(L0_attn_ff_b1, 32),
                           _pk(L0_attn_ff_b2)], axis=1).astype(np.float32)
    zeros_h = np.zeros((128, 8), np.float32)
    ncB1 = build_B1(n)
    imB1 = []
    for c in cores:
        h_in = zeros_h if c % 2 == 0 else rA[c - 1]["hfin"]
        imB1.append({"x_t": x_ts[c], "x2p_t": rA[c]["x2p_t"], "g_t": rA[c]["g_t"],
                     "h_in": h_in, "ff1T": ff1T, "ff2T": ff2T, "vecb": vecb,
                     "consts": consts})
    rB1 = run_bass_kernel_spmd(ncB1, imB1, core_ids=cores).results

    WrT = np.ascontiguousarray(np.asarray(L0_ffn_W_r, np.float32).T)
    WkT = np.ascontiguousarray(np.asarray(L0_ffn_W_k, np.float32).T)
    WvT = np.ascontiguousarray(np.asarray(L0_ffn_W_v, np.float32).T)
    qkvT = np.ascontiguousarray(np.asarray(L1_attn_qkv_w, np.float32).T)
    vecb2 = np.concatenate([_pk(L0_ffn_mu_r), _pk(L0_ffn_mu_k),
                            _pk(L1_attn_qkv_b, 24)], axis=1).astype(np.float32)
    pm = np.asarray(L1_attn_projection_matrix, np.float32)
    blkpm = np.zeros((128, 128), np.float32)
    blkpm[0:64, 0:64] = pm.T
    blkpm[64:128, 64:128] = pm.T
    halfones = np.zeros((128, 2), np.float32)
    halfones[0:64, 0] = 1.0
    halfones[64:128, 1] = 1.0
    negsel = np.zeros((2, 128), np.float32)
    negsel[0, 0:64] = -0.5
    negsel[1, 64:128] = -0.5
    ident = np.eye(128, dtype=np.float32)
    mask0 = np.zeros((128, 1), np.float32)
    mask1 = np.ones((128, 1), np.float32)
    ncB2 = build_B2(n)
    imB2 = [{"xs1_t": rB1[c]["xs1_t"], "WrT": WrT, "WkT": WkT, "WvT": WvT,
             "qkvT": qkvT, "blkpm": blkpm, "vecb2": vecb2,
             "maskz": (mask0 if c % 2 == 0 else mask1), "consts": consts,
             "ident": ident, "halfones": halfones, "negsel": negsel}
            for c in cores]
    rB2 = run_bass_kernel_spmd(ncB2, imB2, core_ids=cores).results

    projT = np.ascontiguousarray(np.asarray(L1_attn_proj_w, np.float32).T)
    wg = np.asarray(L1_ffn_Wg, np.float32)
    wg_pk = np.ascontiguousarray(wg.reshape(8, 128, 8).transpose(1, 0, 2).reshape(128, 64))
    sel8 = np.zeros((8, 1024), np.float32)
    for e_ in range(8):
        sel8[e_, 128 * e_:128 * (e_ + 1)] = 1.0
    W1m = np.asarray(L1_ffn_W1, np.float32)
    W3m = np.asarray(L1_ffn_W3, np.float32)
    W1h = np.ascontiguousarray(
        W1m.reshape(8, 8, 128, 32, 128).transpose(0, 3, 2, 1, 4).reshape(8, 32, 128, 1024))
    W3h = np.ascontiguousarray(
        W3m.reshape(8, 8, 128, 32, 128).transpose(0, 3, 2, 1, 4).reshape(8, 32, 128, 1024))
    W2f = np.ascontiguousarray(np.asarray(L1_ffn_W2, np.float32).reshape(8 * 4096, 1024))
    bgrow = np.ascontiguousarray(np.asarray(L1_ffn_bg, np.float32).reshape(1, 8))
    vecc = _pk(L1_attn_proj_b)
    ncC = build_C(n)
    imC = []
    for c in cores:
        peer = c + 1 if c % 2 == 0 else c - 1
        imC.append({"xs2_t": rB2[c]["xs2_t"], "qp_t": rB2[c]["qp_t"],
                    "kvp_a": rB2[c]["kvp"], "kvp_b": rB2[peer]["kvp"],
                    "projT": projT, "Wg": wg_pk, "bgrow": bgrow, "sel8": sel8,
                    "W1h": W1h, "W3h": W3h, "W2": W2f, "vecc": vecc,
                    "ident": ident, "consts": consts})
    rC = run_bass_kernel_spmd(ncC, imC, core_ids=cores).results

    out = np.empty((B, T, D), np.float32)
    for c in cores:
        b, hf = c // 2, c % 2
        out[b, hf * n:(hf + 1) * n] = rC[c]["xout_t"].T
    return out


# version 07
# speedup vs baseline: 1.6473x; 1.6473x over previous
import contextlib
import numpy as np
import concourse.bass as bass
import concourse.mybir as mybir

F32 = mybir.dt.float32
BF16 = mybir.dt.bfloat16
AF = mybir.ActivationFunctionType
ALU = mybir.AluOpType
AX = mybir.AxisListType.X

P = 128
NDT = 8
NC = 512


class BB:

    def __init__(self, nc, stack):
        self.nc = nc
        self.stack = stack
        self.block = stack.enter_context(nc.Block())
        self.sems = {}
        self.cnt = {}

    def sem(self, name):
        if name not in self.sems:
            self.sems[name] = self.stack.enter_context(self.nc.semaphore(name))
            self.cnt[name] = 0
        return self.sems[name]

    def inc(self, instr, name, n=1):
        instr.then_inc(self.sem(name), n)
        self.cnt[name] += n
        return instr

    def val(self, name):
        self.sem(name)
        return self.cnt[name]

    def wait(self, eng, name, v=None):
        eng.wait_ge(self.sem(name), self.cnt[name] if v is None else v)

    def V(self, f): self.block.vector(f)
    def T(self, f): self.block.tensor(f)
    def S(self, f): self.block.scalar(f)
    def G(self, f): self.block.gpsimd(f)
    def Y(self, f): self.block.sync(f)

    def barrier(self, engines="vtsgy"):
        names = {"v": "bar_v", "t": "bar_t", "s": "bar_s", "g": "bar_g", "y": "bar_y"}
        eng = {"v": self.V, "t": self.T, "s": self.S, "g": self.G, "y": self.Y}
        for e in engines:
            def _f(E, nm=names[e]):
                self.inc(E.nop(), nm)
            eng[e](_f)
        vals = {e: self.val(names[e]) for e in engines}
        for e in engines:
            def _f(E, me=e):
                for o in engines:
                    if o != me:
                        E.wait_ge(self.sem(names[o]), vals[o])
            eng[e](_f)


def col_groups(w, base=0):
    out = []
    a = 0
    while a < w:
        b = min(a + 512 - ((base + a) % 512), w)
        out.append((base + a, base + b))
        a = b
    return out


def dslice(T, j, a, b, W):
    return T[:, j * W + a: j * W + b]



def emit_ln_stats(bb, X, XW, xa, xb, sq, SQW, ps_stat, rows, onescol, onesrow,
                  ps_mb, ps_rb, tag):
    w = xb - xa
    Wr = rows.shape[1] // 4

    def _s(S):
        bb.wait(S, "v")
        for j in range(NDT):
            ins = S.activation(out=dslice(sq, j, xa, xb, SQW),
                               in_=dslice(X, j, xa, xb, XW), func=AF.Square)
        bb.inc(ins, "s")
    bb.S(_s)

    def _t(T):
        bb.wait(T, "s")
        bb.wait(T, "v")
        for (a, b) in col_groups(w, xa):
            for j in range(NDT):
                T.matmul(ps_stat[0:1, a:b], onescol[:], dslice(X, j, a, b, XW),
                         start=(j == 0), stop=(j == NDT - 1))
        for (a, b) in col_groups(w, xa):
            for j in range(NDT):
                ins = T.matmul(ps_stat[32:33, a:b], onescol[:],
                               dslice(sq, j, a, b, SQW),
                               start=(j == 0), stop=(j == NDT - 1))
        bb.inc(ins, "t")
    bb.T(_t)

    def _v(V):
        bb.wait(V, "t")
        V.tensor_scalar_mul(rows[0:1, 0:w], ps_stat[0:1, xa:xb], 1.0 / 1024.0)
        V.tensor_scalar_mul(rows[0:1, Wr:Wr + w], ps_stat[32:33, xa:xb], 1.0 / 1024.0)
        V.tensor_tensor(rows[0:1, 2 * Wr:2 * Wr + w], rows[0:1, 0:w], rows[0:1, 0:w], ALU.mult)
        V.tensor_tensor(rows[0:1, Wr:Wr + w], rows[0:1, Wr:Wr + w],
                        rows[0:1, 2 * Wr:2 * Wr + w], ALU.subtract)
        ins = V.tensor_scalar_add(rows[0:1, Wr:Wr + w], rows[0:1, Wr:Wr + w], 1e-5)
        bb.inc(ins, "v")
    bb.V(_v)

    def _s2(S):
        bb.wait(S, "v")
        ins = S.activation(out=rows[0:1, 2 * Wr:2 * Wr + w], in_=rows[0:1, Wr:Wr + w],
                           func=AF.Sqrt)
        bb.inc(ins, "s")
    bb.S(_s2)

    def _v2(V):
        bb.wait(V, "s")
        ins = V.reciprocal(rows[0:1, 3 * Wr:3 * Wr + w], rows[0:1, 2 * Wr:2 * Wr + w])
        bb.inc(ins, "v")
    bb.V(_v2)

    def _t2(T):
        bb.wait(T, "v")
        for (a, b) in col_groups(w, xa):
            T.matmul(ps_mb[:, a:b], onesrow[0:1, :], rows[0:1, a - xa:b - xa],
                     start=True, stop=True)
            ins = T.matmul(ps_rb[:, a:b], onesrow[0:1, :],
                           rows[0:1, 3 * Wr + a - xa:3 * Wr + b - xa],
                           start=True, stop=True)
        bb.inc(ins, "t")
    bb.T(_t2)


def emit_ln_apply(bb, X, XW, OUT, OUTW, xa, xb, oa, ps_mb, ps_rb, mid=None):
    w = xb - xa

    def _v(V):
        bb.wait(V, "t")
        if mid is None:
            MID, MIDW, ma = OUT, OUTW, oa
        else:
            MID, MIDW = mid
            ma = 0
        for j in range(NDT):
            V.tensor_tensor(dslice(MID, j, ma, ma + w, MIDW),
                            dslice(X, j, xa, xb, XW), ps_mb[:, xa:xb], ALU.subtract)
        for j in range(NDT):
            ins = V.tensor_tensor(dslice(OUT, j, oa, oa + w, OUTW),
                                  dslice(MID, j, ma, ma + w, MIDW), ps_rb[:, xa:xb], ALU.mult)
        bb.inc(ins, "v")
    bb.V(_v)


def emit_rms(bb, X, XW, OUT, OUTW, xa, xb, oa, sq, SQW, ps_stat, rows,
             onescol, ones32row, ps_bc, gp1, tag, tmp=None):
    w = xb - xa
    Wr = rows.shape[1] // 4

    def _s(S):
        bb.wait(S, "v")
        for j in range(NDT):
            ins = S.activation(out=dslice(sq, j, xa, xb, SQW),
                               in_=dslice(X, j, xa, xb, XW), func=AF.Square)
        bb.inc(ins, "s")
    bb.S(_s)

    def _t(T):
        bb.wait(T, "s")
        for (a, b) in col_groups(w, xa):
            for j in range(NDT):
                ins = T.matmul(ps_stat[0:1, a:b], onescol[:], dslice(sq, j, a, b, SQW),
                               start=(j == 0), stop=(j == NDT - 1))
        bb.inc(ins, "t")
    bb.T(_t)

    def _s2(S):
        bb.wait(S, "t")
        ins = S.activation(out=rows[0:1, 0:w], in_=ps_stat[0:1, xa:xb], func=AF.Sqrt)
        bb.inc(ins, "s")
    bb.S(_s2)

    def _v(V):
        bb.wait(V, "s")
        bb.inc(V.tensor_scalar_max(rows[0:1, Wr:Wr + w], rows[0:1, 0:w], 1e-12), "v")
        bb.wait(V, "v")
        ins = V.reciprocal(rows[0:1, 2 * Wr:2 * Wr + w], rows[0:1, Wr:Wr + w])
        bb.inc(ins, "v")
    bb.V(_v)

    def _t2(T):
        bb.wait(T, "v")
        for (a, b) in col_groups(w, xa):
            ins = T.matmul(ps_bc[:, a:b], ones32row[0:1, :],
                           rows[0:1, 2 * Wr + a - xa:2 * Wr + b - xa], start=True, stop=True)
        bb.inc(ins, "t")
    bb.T(_t2)

    def _v2(V):
        bb.wait(V, "t")
        if tmp is None:
            MID, MIDW, ma = OUT, OUTW, oa
        else:
            MID, MIDW = tmp
            ma = 0
        for j in range(NDT):
            V.tensor_tensor(dslice(MID, j, ma, ma + w, MIDW),
                            dslice(X, j, xa, xb, XW), ps_bc[:, xa:xb], ALU.mult)
        for j in range(NDT):
            ins = V.tensor_scalar_mul(dslice(OUT, j, oa, oa + w, OUTW),
                                      dslice(MID, j, ma, ma + w, MIDW), gp1[:, j:j + 1])
        bb.inc(ins, "v")
    bb.V(_v2)


def emit_mm(bb, ps, wsb, WW, rhs, RHSW, ra, rb_, out_et, n_kt=NDT, wait=("v", "s"),
            extra_lhsT=None, extra_rhs=None):
    def _t(T):
        for s in wait:
            bb.wait(T, s)
        for (a, b) in col_groups(rb_ - ra, ra):
            for kt in range(n_kt):
                ins = T.matmul(ps[:, a:b],
                               wsb[:, kt * WW + 128 * out_et: kt * WW + 128 * out_et + 128],
                               dslice(rhs, kt, a, b, RHSW),
                               start=(kt == 0),
                               stop=(kt == n_kt - 1 and extra_lhsT is None))
            if extra_lhsT is not None:
                ins = T.matmul(ps[:, a:b], extra_lhsT, extra_rhs(a, b),
                               start=False, stop=True)
        bb.inc(ins, "t")
    bb.T(_t)



def emit_stageA(bb, nc, st, io, n):
    e = st.enter_context
    W = 517
    WL = 515
    WO = 513

    xc = e(nc.sbuf_tensor("A_xc", [P, NDT * WL], F32))
    sq = e(nc.sbuf_tensor("A_sq", [P, NDT * WL], F32))
    ln0 = e(nc.sbuf_tensor("A_ln0", [P, NDT * WL], F32))
    r0 = e(nc.sbuf_tensor("A_r0", [P, NDT * W], F32))
    x1 = e(nc.sbuf_tensor("A_x1", [P, NDT * WO], F32))
    hid = e(nc.sbuf_tensor("A_hid", [P, NDT * WO], F32))
    at = e(nc.sbuf_tensor("A_a", [P, NDT * WO], F32))
    ht = e(nc.sbuf_tensor("A_h", [P, NDT * WO], F32))
    ybf = e(nc.sbuf_tensor("A_ybf", [P, NDT * WO], BF16))
    r1bf = e(nc.sbuf_tensor("A_r1bf", [P, NDT * WO], BF16))
    zer = e(nc.sbuf_tensor("A_zer", [P, WO], F32))
    rows = e(nc.sbuf_tensor("A_rows", [1, 4 * WL], F32))
    wpw = e(nc.sbuf_tensor("A_wpw", [P, NDT * 1024], BF16))
    wgru = e(nc.sbuf_tensor("A_wgru", [P, NDT * 2048], BF16))
    veca = e(nc.sbuf_tensor("A_veca", [P, 64], F32))
    gp1c = e(nc.sbuf_tensor("A_gp1c", [P, NDT], F32))
    gp1g = e(nc.sbuf_tensor("A_gp1g", [P, NDT], F32))
    cb = e(nc.sbuf_tensor("A_cb", [P, NDT], F32))
    hcar = e(nc.sbuf_tensor("A_hcar", [P, NDT], F32))
    gcar = e(nc.sbuf_tensor("A_gcar", [P, NDT], F32))
    consts = e(nc.sbuf_tensor("A_consts", [P, 4], F32))
    onescol = e(nc.sbuf_tensor("A_ones", [P, 1], F32))
    onesrow = e(nc.sbuf_tensor("A_onesr", [1, P], F32))
    ones32 = e(nc.sbuf_tensor("A_ones32", [1, P], F32))

    ps_stat = e(nc.psum_tensor("A_pstat", [33, 1024], F32))
    ps_b1 = e(nc.psum_tensor("A_psb1", [P, 1024], F32))
    ps_b2 = e(nc.psum_tensor("A_psb2", [P, 1024], F32))
    ps_mm = e(nc.psum_tensor("A_psmm", [P, 1024], F32))

    C_CG, C_GG, C_DB, C_W0, C_W1, C_W2, C_PB = 0, 8, 16, 24, 32, 40, 48

    def _g(Gp):
        for j in range(NDT):
            bb.inc(Gp.dma_start(out=wpw[:, j * 1024:(j + 1) * 1024],
                                in_=io["pwT"][j * 128:(j + 1) * 128, :]), "gd", 16)
        for j in range(NDT):
            bb.inc(Gp.dma_start(out=wgru[:, j * 2048:(j + 1) * 2048],
                                in_=io["gruT"][j * 128:(j + 1) * 128, :]), "gd", 16)
    bb.G(_g)

    def _y(Y):
        bb.inc(Y.dma_start(out=veca[:, 0:56], in_=io["veca"][:]), "yd", 16)
        bb.inc(Y.dma_start(out=consts[:], in_=io["consts"][:]), "yd", 16)
    bb.Y(_y)
    nc.const_aps.register(F32, -1.0, consts[:, 0:1])
    nc.const_aps.register(F32, 0.125, consts[:, 1:2])
    nc.const_aps.register(F32, 1e-6, consts[:, 2:3])

    def _v(V):
        bb.wait(V, "yd")
        V.memset(onescol[:], 1.0)
        V.memset(onesrow[:], 1.0)
        V.memset(ones32[:], 32.0)
        V.memset(zer[:], 0.0)
        V.memset(hcar[:], 0.0)
        V.memset(gcar[:], 1.0)
        V.tensor_scalar_add(gp1c[:], veca[:, C_CG:C_CG + 8], 1.0)
        ins = V.tensor_scalar_add(gp1g[:], veca[:, C_GG:C_GG + 8], 1.0)
        bb.inc(ins, "v")
    bb.V(_v)

    dwbf = e(nc.sbuf_tensor("A_dwbf", [P, NDT], BF16))

    def _v2(V):
        bb.wait(V, "v")
        ins = V.tensor_copy(dwbf[:], veca[:, C_DB:C_DB + 8])
        bb.inc(ins, "v")
    bb.V(_v2)

    def _t_fix(T):
        bb.wait(T, "v")
        bb.wait(T, "gd")
        for et in range(NDT):
            for j in range(NDT):
                ins = T.matmul(ps_mm[:, 512 + et:512 + et + 1],
                               wpw[:, j * 1024 + 128 * et: j * 1024 + 128 * et + 128],
                               dwbf[:, j:j + 1],
                               start=(j == 0), stop=(j == NDT - 1))
        bb.inc(ins, "t")
    bb.T(_t_fix)

    def _v3(V):
        bb.wait(V, "t")
        ins = V.tensor_tensor(cb[:], ps_mm[:, 512:520], veca[:, C_PB:C_PB + 8], ALU.add)
        bb.inc(ins, "v")
    bb.V(_v3)

    def _tw(T):
        bb.wait(T, "gd")
        bb.wait(T, "yd")
    bb.T(_tw)
    bb.barrier("vtsy")

    nchunks = n // NC
    for c in range(nchunks):
        if c == 0:
            xa, xw = 0, WL
            fr_a, fr_w = 0, WL
            oa_ln = 2
            wo = WO
            ocol = 0
        else:
            xa, xw = 3 + c * NC - 0, NC
            fr_a, fr_w = 2, NC
            oa_ln = 0
            wo = NC
            ocol = 1 + c * NC

        def _yl(Y, c=c, xa=xa, xw=xw):
            if c > 0:
                bb.wait(Y, "v")
            for j in range(NDT):
                bb.inc(Y.dma_start(out=dslice(xc, j, 0, xw, WL),
                                   in_=io["x_t"][j * 128:(j + 1) * 128, xa:xa + xw]), "yd", 16)
        bb.Y(_yl)

        def _vw(V):
            bb.wait(V, "yd")
            bb.inc(V.nop(), "v")
        bb.V(_vw)

        if c >= 1:
            src = 513 if c == 1 else 512
            def _vc(V, src=src):
                for j in range(NDT):
                    ins = V.tensor_copy(dslice(r0, j, 0, 2, W), dslice(r0, j, src, src + 2, W))
                bb.inc(ins, "v")
            bb.V(_vc)

        emit_ln_stats(bb, xc, WL, 0, xw, sq, WL, ps_stat, rows, onescol, onesrow,
                      ps_b1, ps_b2, f"ln0_{c}")
        emit_ln_apply(bb, xc, WL, ln0, WL, 0, xw, 0, ps_b1, ps_b2)
        emit_rms(bb, ln0, WL, r0, W, 0, xw, fr_a, sq, WL, ps_stat, rows,
                 onescol, ones32, ps_b1, gp1c, f"r0_{c}")

        def _vconv(V, wo=wo, fr_a=fr_a, c=c):
            bb.wait(V, "v")
            base = 0 if c == 0 else 0
            for j in range(NDT):
                V.tensor_scalar_mul(dslice(hid, j, 0, wo, WO),
                                    dslice(r0, j, base, base + wo, W), veca[:, C_W0 + j:C_W0 + j + 1])
            for j in range(NDT):
                V.scalar_tensor_tensor(dslice(hid, j, 0, wo, WO),
                                       dslice(r0, j, base + 1, base + 1 + wo, W),
                                       veca[:, C_W1 + j:C_W1 + j + 1],
                                       dslice(hid, j, 0, wo, WO), ALU.mult, ALU.add)
            for j in range(NDT):
                ins = V.scalar_tensor_tensor(dslice(ybf, j, 0, wo, WO),
                                             dslice(r0, j, base + 2, base + 2 + wo, W),
                                             veca[:, C_W2 + j:C_W2 + j + 1],
                                             dslice(hid, j, 0, wo, WO), ALU.mult, ALU.add)
            bb.inc(ins, "v")
        bb.V(_vconv)

        for et in range(NDT):
            emit_mm(bb, ps_mm, wpw, 1024, ybf, WO, 0, wo, et)
            def _vev(V, et=et, wo=wo, oa_ln=oa_ln):
                bb.wait(V, "t")
                ins = V.scalar_tensor_tensor(dslice(x1, et, 0, wo, WO),
                                             ps_mm[:, 0:wo], cb[:, et:et + 1],
                                             dslice(ln0, et, oa_ln, oa_ln + wo, WL),
                                             ALU.add, ALU.add)
                bb.inc(ins, "v")
            bb.V(_vev)

        emit_rms(bb, x1, WO, r1bf, WO, 0, wo, 0, sq, WL, ps_stat, rows,
                 onescol, ones32, ps_b1, gp1g, f"r1_{c}", tmp=(hid, WO))

        for et in range(NDT):
            emit_mm(bb, ps_mm, wgru, 2048, r1bf, WO, 0, wo, et)
            def _vh(V, et=et, wo=wo):
                bb.wait(V, "t")
                ins = V.tensor_copy(dslice(hid, et, 0, wo, WO), ps_mm[:, 0:wo])
                bb.inc(ins, "v")
            bb.V(_vh)
        for et in range(NDT):
            emit_mm(bb, ps_mm, wgru, 2048, r1bf, WO, 0, wo, et + 8)
            def _sg(S, et=et, wo=wo):
                bb.wait(S, "t")
                S.activation(out=dslice(at, et, 0, wo, WO), in_=ps_mm[:, 0:wo],
                             func=AF.Sigmoid, scale=-1.0)
                S.activation(out=dslice(sq, et, 0, wo, WL), in_=ps_mm[:, 0:wo],
                             func=AF.Sigmoid)
                ins = S.activation(out=dslice(ln0, et, 0, wo, WL),
                                   in_=dslice(hid, et, 0, wo, WO), func=AF.Sigmoid)
                bb.inc(ins, "s")
            bb.S(_sg)
            def _vb(V, et=et, wo=wo):
                bb.wait(V, "s")
                V.tensor_scalar_min(dslice(ln0, et, 0, wo, WL),
                                    dslice(ln0, et, 0, wo, WL), 0.5)
                V.tensor_scalar_max(dslice(hid, et, 0, wo, WO),
                                    dslice(hid, et, 0, wo, WO), 0.0)
                V.tensor_tensor(dslice(ln0, et, 0, wo, WL), dslice(ln0, et, 0, wo, WL),
                                dslice(hid, et, 0, wo, WO), ALU.add)
                ins = V.tensor_tensor(dslice(sq, et, 0, wo, WL), dslice(sq, et, 0, wo, WL),
                                      dslice(ln0, et, 0, wo, WL), ALU.mult)
                bb.inc(ins, "v")
            bb.V(_vb)

        def _vscan(V, c=c, wo=wo):
            bb.wait(V, "v")
            sa = 1 if c == 0 else 0
            if c == 0:
                for j in range(NDT):
                    V.memset(dslice(ht, j, 0, 1, WO), 0.0)
                    V.memset(dslice(ln0, j, 0, 1, WL), 1.0)
            for j in range(NDT):
                V.tensor_tensor_scan(out=dslice(ht, j, sa, wo, WO),
                                     data0=dslice(at, j, sa, wo, WO),
                                     data1=dslice(sq, j, sa, wo, WL),
                                     initial=hcar[:, j:j + 1], op0=ALU.mult, op1=ALU.add)
            for j in range(NDT):
                V.tensor_tensor_scan(out=dslice(ln0, j, sa, wo, WL),
                                     data0=dslice(at, j, sa, wo, WO),
                                     data1=zer[:, sa:wo],
                                     initial=gcar[:, j:j + 1], op0=ALU.mult, op1=ALU.add)
            for j in range(NDT):
                V.tensor_copy(hcar[:, j:j + 1], dslice(ht, j, wo - 1, wo, WO))
                V.tensor_copy(gcar[:, j:j + 1], dslice(ln0, j, wo - 1, wo, WL))
            for j in range(NDT):
                ins = V.tensor_tensor(dslice(ht, j, 0, wo, WO), dslice(ht, j, 0, wo, WO),
                                      dslice(x1, j, 0, wo, WO), ALU.add)
            bb.inc(ins, "v")
        bb.V(_vscan)

        def _yst(Y, wo=wo, ocol=ocol):
            bb.wait(Y, "v")
            for j in range(NDT):
                bb.inc(Y.dma_start(out=io["x2p_t"][j * 128:(j + 1) * 128, ocol:ocol + wo],
                                   in_=dslice(ht, j, 0, wo, WO)), "st", 16)
            for j in range(NDT):
                bb.inc(Y.dma_start(out=io["g_t"][j * 128:(j + 1) * 128, ocol:ocol + wo],
                                   in_=dslice(ln0, j, 0, wo, WL)), "st", 16)
        bb.Y(_yst)

        bb.barrier("vtsy")

    def _yf(Y):
        ins = Y.dma_start(out=io["hfin"][:], in_=hcar[:])
        bb.inc(ins, "st", 16)
        bb.wait(Y, "st")
    bb.Y(_yf)


def build_A(n):
    nc = bass.Bass("TRN2", target_bir_lowering=False, detect_race_conditions=False)
    io = {}
    io["x_t"] = nc.dram_tensor("x_t", [1024, 3 + n], F32, kind="ExternalInput").ap()
    io["pwT"] = nc.dram_tensor("pwT", [1024, 1024], F32, kind="ExternalInput").ap()
    io["gruT"] = nc.dram_tensor("gruT", [1024, 2048], F32, kind="ExternalInput").ap()
    io["veca"] = nc.dram_tensor("veca", [128, 56], F32, kind="ExternalInput").ap()
    io["consts"] = nc.dram_tensor("consts", [128, 4], F32, kind="ExternalInput").ap()
    io["x2p_t"] = nc.dram_tensor("x2p_t", [1024, 1 + n], F32, kind="ExternalOutput").ap()
    io["g_t"] = nc.dram_tensor("g_t", [1024, 1 + n], F32, kind="ExternalOutput").ap()
    io["hfin"] = nc.dram_tensor("hfin", [128, 8], F32, kind="ExternalOutput").ap()
    with contextlib.ExitStack() as st:
        bb = BB(nc, st)
        emit_stageA(bb, nc, st, io, n)
    return nc



def emit_ln_apply_res(bb, X, XW, OUT, OUTW, RES, RESW, xa, xb, oa, ra,
                      ps_mb, ps_rb, mid, MIDW):
    w = xb - xa

    def _v(V):
        bb.wait(V, "t")
        for j in range(NDT):
            V.tensor_tensor(dslice(mid, j, 0, w, MIDW),
                            dslice(X, j, xa, xb, XW), ps_mb[:, 0:w], ALU.subtract)
        for j in range(NDT):
            V.tensor_tensor(dslice(mid, j, 0, w, MIDW),
                            dslice(mid, j, 0, w, MIDW), ps_rb[:, 0:w], ALU.mult)
        for j in range(NDT):
            ins = V.tensor_tensor(dslice(OUT, j, oa, oa + w, OUTW),
                                  dslice(mid, j, 0, w, MIDW),
                                  dslice(RES, j, ra, ra + w, RESW), ALU.add)
            bb.inc(ins, "v")
    bb.V(_v)


NCB = 256

def emit_stageB1(bb, nc, st, io, n):
    e = st.enter_context
    W = NCB
    NHT = 32

    xq = e(nc.sbuf_tensor("B_xq", [P, NDT * W], F32))
    gq = e(nc.sbuf_tensor("B_gq", [P, NDT * W], F32))
    sq = e(nc.sbuf_tensor("B_sq", [P, NDT * W], F32))
    x3 = e(nc.sbuf_tensor("B_x3", [P, NDT * W], F32))
    xr = e(nc.sbuf_tensor("B_xr", [P, NDT * W], F32))
    r2bf = e(nc.sbuf_tensor("B_r2bf", [P, NDT * W], BF16))
    h1bf = e(nc.sbuf_tensor("B_h1bf", [P, NHT * W], BF16))
    w1sb = e(nc.sbuf_tensor("B_w1", [P, NDT * 4096], BF16))
    w2sb = e(nc.sbuf_tensor("B_w2", [P, NHT * 1024], BF16))
    rows = e(nc.sbuf_tensor("B_rows", [1, 4 * W], F32))
    vecb = e(nc.sbuf_tensor("B_vecb", [P, 48], F32))
    gp1f = e(nc.sbuf_tensor("B_gp1f", [P, NDT], F32))
    hin = e(nc.sbuf_tensor("B_hin", [P, NDT], F32))
    consts = e(nc.sbuf_tensor("B_consts", [P, 4], F32))
    onescol = e(nc.sbuf_tensor("B_ones", [P, 1], F32))
    onesrow = e(nc.sbuf_tensor("B_onesr", [1, P], F32))
    ones32 = e(nc.sbuf_tensor("B_ones32", [1, P], F32))

    ps_stat = e(nc.psum_tensor("B_pstat", [33, 512], F32))
    ps_b1 = e(nc.psum_tensor("B_psb1", [P, 512], F32))
    ps_b2 = e(nc.psum_tensor("B_psb2", [P, 512], F32))
    ps_mm = e(nc.psum_tensor("B_psmm", [P, 512], F32))

    nc.const_aps.register(F32, -1.0, consts[:, 0:1])
    nc.const_aps.register(F32, 0.125, consts[:, 1:2])
    nc.const_aps.register(F32, 1e-6, consts[:, 2:3])

    C_FG, C_B1, C_B2 = 0, 8, 40

    def _g(Gp):
        for j in range(NDT):
            bb.inc(Gp.dma_start(out=w1sb[:, j * 4096:(j + 1) * 4096],
                                in_=io["ff1T"][j * 128:(j + 1) * 128, :]), "gd", 16)
        for j in range(NHT):
            bb.inc(Gp.dma_start(out=w2sb[:, j * 1024:(j + 1) * 1024],
                                in_=io["ff2T"][j * 128:(j + 1) * 128, :]), "gd", 16)
    bb.G(_g)

    def _y(Y):
        bb.inc(Y.dma_start(out=vecb[:], in_=io["vecb"][:]), "yd", 16)
        bb.inc(Y.dma_start(out=hin[:], in_=io["h_in"][:]), "yd", 16)
        bb.inc(Y.dma_start(out=consts[:], in_=io["consts"][:]), "yd", 16)
    bb.Y(_y)

    def _v(V):
        bb.wait(V, "yd")
        V.memset(onescol[:], 1.0)
        V.memset(onesrow[:], 1.0)
        V.memset(ones32[:], 32.0)
        ins = V.tensor_scalar_add(gp1f[:], vecb[:, C_FG:C_FG + 8], 1.0)
        bb.inc(ins, "v")
    bb.V(_v)

    def _tw(T):
        bb.wait(T, "gd")
        bb.wait(T, "yd")
    bb.T(_tw)
    bb.barrier("vtsy")

    chunks = [(0, 2, 1, 0)]
    for c in range(n // NCB):
        chunks.append((1 + c * NCB, 3 + c * NCB, NCB, 1 + c * NCB))

    for (pcol, rcol, wo, ocol) in chunks:
        def _yl(Y, pcol=pcol, wo=wo):
            for j in range(NDT):
                bb.inc(Y.dma_start(out=dslice(xq, j, 0, wo, W),
                                   in_=io["x2p_t"][j * 128:(j + 1) * 128, pcol:pcol + wo],
                                   allow_slow_non_contiguous=(wo == 1)), "yd", 16)
            for j in range(NDT):
                bb.inc(Y.dma_start(out=dslice(gq, j, 0, wo, W),
                                   in_=io["g_t"][j * 128:(j + 1) * 128, pcol:pcol + wo],
                                   allow_slow_non_contiguous=(wo == 1)), "yd", 16)
        bb.Y(_yl)

        def _vf(V, wo=wo):
            bb.wait(V, "yd")
            for j in range(NDT):
                ins = V.scalar_tensor_tensor(dslice(xq, j, 0, wo, W),
                                             dslice(gq, j, 0, wo, W), hin[:, j:j + 1],
                                             dslice(xq, j, 0, wo, W), ALU.mult, ALU.add)
            bb.inc(ins, "v")
        bb.V(_vf)

        emit_rms(bb, xq, W, r2bf, W, 0, wo, 0, sq, W, ps_stat, rows,
                 onescol, ones32, ps_b1, gp1f, "r2", tmp=(gq, W))

        for ht in range(NHT):
            emit_mm(bb, ps_mm, w1sb, 4096, r2bf, W, 0, wo, ht)
            def _sg(S, ht=ht, wo=wo):
                bb.wait(S, "t")
                ins = S.activation(out=dslice(h1bf, ht, 0, wo, W), in_=ps_mm[:, 0:wo],
                                   func=AF.Gelu, bias=vecb[:, C_B1 + ht:C_B1 + ht + 1])
                bb.inc(ins, "s")
            bb.S(_sg)

        for et in range(NDT):
            emit_mm(bb, ps_mm, w2sb, 1024, h1bf, W, 0, wo, et, n_kt=NHT)
            def _vd(V, et=et, wo=wo):
                bb.wait(V, "t")
                ins = V.scalar_tensor_tensor(dslice(x3, et, 0, wo, W),
                                             ps_mm[:, 0:wo], vecb[:, C_B2 + et:C_B2 + et + 1],
                                             dslice(xq, et, 0, wo, W), ALU.add, ALU.add)
                bb.inc(ins, "v")
            bb.V(_vd)

        def _yr(Y, rcol=rcol, wo=wo):
            for j in range(NDT):
                bb.inc(Y.dma_start(out=dslice(xr, j, 0, wo, W),
                                   in_=io["x_t"][j * 128:(j + 1) * 128, rcol:rcol + wo],
                                   allow_slow_non_contiguous=(wo == 1)), "yd", 16)
        bb.Y(_yr)

        emit_ln_stats(bb, x3, W, 0, wo, sq, W, ps_stat, rows, onescol, onesrow,
                      ps_b1, ps_b2, "ln3")
        def _vw(V):
            bb.wait(V, "yd")
            bb.inc(V.nop(), "v")
        bb.V(_vw)
        emit_ln_apply_res(bb, x3, W, x3, W, xr, W, 0, wo, 0, 0, ps_b1, ps_b2, gq, W)

        def _yst(Y, wo=wo, ocol=ocol):
            bb.wait(Y, "v")
            for j in range(NDT):
                bb.inc(Y.dma_start(out=io["xs1_t"][j * 128:(j + 1) * 128, ocol:ocol + wo],
                                   in_=dslice(x3, j, 0, wo, W),
                                   allow_slow_non_contiguous=(wo == 1)), "st", 16)
        bb.Y(_yst)

        bb.barrier("vtsy")

    def _yf(Y):
        bb.wait(Y, "st")
    bb.Y(_yf)


def build_B1(n):
    nc = bass.Bass("TRN2", target_bir_lowering=False, detect_race_conditions=False)
    io = {}
    io["x_t"] = nc.dram_tensor("x_t", [1024, 3 + n], F32, kind="ExternalInput").ap()
    io["x2p_t"] = nc.dram_tensor("x2p_t", [1024, 1 + n], F32, kind="ExternalInput").ap()
    io["g_t"] = nc.dram_tensor("g_t", [1024, 1 + n], F32, kind="ExternalInput").ap()
    io["h_in"] = nc.dram_tensor("h_in", [128, 8], F32, kind="ExternalInput").ap()
    io["ff1T"] = nc.dram_tensor("ff1T", [1024, 4096], F32, kind="ExternalInput").ap()
    io["ff2T"] = nc.dram_tensor("ff2T", [4096, 1024], F32, kind="ExternalInput").ap()
    io["vecb"] = nc.dram_tensor("vecb", [128, 48], F32, kind="ExternalInput").ap()
    io["consts"] = nc.dram_tensor("consts", [128, 4], F32, kind="ExternalInput").ap()
    io["xs1_t"] = nc.dram_tensor("xs1_t", [1024, 1 + n], F32, kind="ExternalOutput").ap()
    with contextlib.ExitStack() as st:
        bb = BB(nc, st)
        emit_stageB1(bb, nc, st, io, n)
    return nc



def emit_stageB2(bb, nc, st, io, n):
    e = st.enter_context
    W = 256
    ZW = 257
    nch = n // W

    xs1c = e(nc.sbuf_tensor("C_xs1", [P, NDT * W], F32))
    z = e(nc.sbuf_tensor("C_z", [P, NDT * ZW], F32))
    sq = e(nc.sbuf_tensor("C_sq", [P, NDT * W], F32))
    kvsb = e(nc.sbuf_tensor("C_kv", [P, NDT * W], F32))
    xrbf = e(nc.sbuf_tensor("C_xr", [P, NDT * W], BF16))
    xkbf = e(nc.sbuf_tensor("C_xk", [P, NDT * W], BF16))
    k2bf = e(nc.sbuf_tensor("C_k2", [P, NDT * W], BF16))
    ln4bf = e(nc.sbuf_tensor("C_ln4", [P, NDT * W], BF16))
    qpc = e(nc.sbuf_tensor("C_qpc", [P, NDT * W], BF16))
    scr = e(nc.sbuf_tensor("C_scr", [P, W], F32))
    scr2 = e(nc.sbuf_tensor("C_scr2", [P, W], F32))
    qdn = e(nc.sbuf_tensor("C_qdn", [P, W], BF16))
    kdn = e(nc.sbuf_tensor("C_kdn", [P, W], BF16))
    combsb = e(nc.sbuf_tensor("C_comb", [P, W], F32))
    nsq = e(nc.sbuf_tensor("C_nsq", [2, W], F32))
    qptm = e(nc.sbuf_tensor("C_qptm", [P, 128], F32))
    kptm = e(nc.sbuf_tensor("C_kptm", [P, 128], BF16))
    vtm = e(nc.sbuf_tensor("C_vtm", [P, 128], BF16))
    negm = e(nc.sbuf_tensor("C_negm", [P, 4], F32))
    rows = e(nc.sbuf_tensor("C_rows", [1, 4 * W], F32))
    wr = e(nc.sbuf_tensor("C_wr", [P, NDT * 1024], BF16))
    wk = e(nc.sbuf_tensor("C_wk", [P, NDT * 1024], BF16))
    wv = e(nc.sbuf_tensor("C_wv", [P, NDT * 1024], BF16))
    wqkv = e(nc.sbuf_tensor("C_wqkv", [P, NDT * 3072], BF16))
    blkpm = e(nc.sbuf_tensor("C_blkpm", [P, 128], BF16))
    ident = e(nc.sbuf_tensor("C_ident", [P, 128], F32))
    halfones = e(nc.sbuf_tensor("C_half", [P, 2], F32))
    negsel = e(nc.sbuf_tensor("C_negsel", [2, 128], F32))
    vecb2 = e(nc.sbuf_tensor("C_vec", [P, 40], F32))
    qb8 = e(nc.sbuf_tensor("C_qb8", [P, 16], F32))
    maskz = e(nc.sbuf_tensor("C_maskz", [P, 1], F32))
    consts = e(nc.sbuf_tensor("C_consts", [P, 4], F32))
    onescol = e(nc.sbuf_tensor("C_ones", [P, 1], F32))
    onescbf = e(nc.sbuf_tensor("C_onesbf", [P, 1], BF16))
    zer64 = e(nc.sbuf_tensor("C_zer64", [P, 64], BF16))
    onesrow = e(nc.sbuf_tensor("C_onesr", [1, P], F32))
    ones32 = e(nc.sbuf_tensor("C_ones32", [1, P], F32))
    kvout = e(nc.sbuf_tensor("C_kvout", [64, 1040], F32))

    ps_stat = e(nc.psum_tensor("C_pstat", [33, 512], F32))
    ps_b1 = e(nc.psum_tensor("C_psb1", [P, 512], F32))
    ps_b2 = e(nc.psum_tensor("C_psb2", [P, 512], F32))
    ps_mm = e(nc.psum_tensor("C_psmm", [P, 512], F32))
    ps_misc = e(nc.psum_tensor("C_psmisc", [P, 512], F32))
    ps_kv = e(nc.psum_tensor("C_pskv", [64, 1024], F32))
    ps_ks = e(nc.psum_tensor("C_psks", [64, 16], F32))

    nc.const_aps.register(F32, -1.0, consts[:, 0:1])
    nc.const_aps.register(F32, 0.125, consts[:, 1:2])
    nc.const_aps.register(F32, 1e-6, consts[:, 2:3])

    C_MR, C_MK, C_QB = 0, 8, 16

    def _g(Gp):
        for j in range(NDT):
            bb.inc(Gp.dma_start(out=wr[:, j * 1024:(j + 1) * 1024],
                                in_=io["WrT"][j * 128:(j + 1) * 128, :]), "gd", 16)
        for j in range(NDT):
            bb.inc(Gp.dma_start(out=wk[:, j * 1024:(j + 1) * 1024],
                                in_=io["WkT"][j * 128:(j + 1) * 128, :]), "gd", 16)
        for j in range(NDT):
            bb.inc(Gp.dma_start(out=wv[:, j * 1024:(j + 1) * 1024],
                                in_=io["WvT"][j * 128:(j + 1) * 128, :]), "gd", 16)
        for j in range(NDT):
            bb.inc(Gp.dma_start(out=wqkv[:, j * 3072:(j + 1) * 3072],
                                in_=io["qkvT"][j * 128:(j + 1) * 128, :]), "gd", 16)
        bb.inc(Gp.dma_start(out=blkpm[:], in_=io["blkpm"][:]), "gd", 16)
    bb.G(_g)

    def _y(Y):
        for nm, t in [("vecb2", vecb2), ("maskz", maskz), ("consts", consts),
                      ("ident", ident), ("halfones", halfones), ("negsel", negsel)]:
            bb.inc(Y.dma_start(out=t[:], in_=io[nm][:]), "yd", 16)
    bb.Y(_y)

    def _v(V):
        bb.wait(V, "yd")
        V.memset(onescol[:], 1.0)
        V.memset(onescbf[:], 1.0)
        V.memset(zer64[:], 0.0)
        V.memset(onesrow[:], 1.0)
        V.memset(ones32[:], 32.0)
        ins = V.tensor_scalar_mul(qb8[:], vecb2[:, C_QB:C_QB + 16], 0.125)
        bb.inc(ins, "v")
    bb.V(_v)

    def _tw(T):
        bb.wait(T, "gd")
        bb.wait(T, "yd")
        bb.wait(T, "v")
        T.matmul(ps_kv[0:64, 0:512], zer64[:], wqkv[:, 0:512],
                 start=True, stop=False, skip_group_check=True)
        T.matmul(ps_kv[0:64, 512:1024], zer64[:], wqkv[:, 0:512],
                 start=True, stop=False, skip_group_check=True)
        T.matmul(ps_ks[0:64, 0:16], zer64[:], wqkv[:, 0:16],
                 start=True, stop=False, skip_group_check=True)
    bb.T(_tw)
    bb.barrier("vtsy")

    def _ym(Y):
        for j in range(NDT):
            bb.inc(Y.dma_start(out=dslice(xs1c, j, 0, 1, W),
                               in_=io["xs1_t"][j * 128:(j + 1) * 128, 0:1],
                               allow_slow_non_contiguous=True), "yd", 16)
    bb.Y(_ym)
    def _vm(V):
        bb.wait(V, "yd")
        bb.inc(V.nop(), "v")
    bb.V(_vm)
    emit_ln_stats(bb, xs1c, W, 0, 1, sq, W, ps_stat, rows, onescol, onesrow,
                  ps_b1, ps_b2, "lnzb")
    emit_ln_apply(bb, xs1c, W, z, ZW, 0, 1, 0, ps_b1, ps_b2)
    def _vm2(V):
        for j in range(NDT):
            ins = V.tensor_scalar_mul(dslice(z, j, 0, 1, ZW), dslice(z, j, 0, 1, ZW),
                                      maskz[:, 0:1])
        bb.inc(ins, "v")
    bb.V(_vm2)
    bb.barrier("vtsy")

    for c in range(nch):
        if c > 0:
            def _vz(V):
                for j in range(NDT):
                    ins = V.tensor_copy(dslice(z, j, 0, 1, ZW), dslice(z, j, W, W + 1, ZW))
                bb.inc(ins, "v")
            bb.V(_vz)

        def _yl(Y, c=c):
            bb.wait(Y, "v")
            for j in range(NDT):
                bb.inc(Y.dma_start(out=dslice(xs1c, j, 0, W, W),
                                   in_=io["xs1_t"][j * 128:(j + 1) * 128, 1 + c * W:1 + (c + 1) * W]),
                       "yd", 16)
        bb.Y(_yl)
        def _vw(V):
            bb.wait(V, "yd")
            bb.inc(V.nop(), "v")
        bb.V(_vw)

        emit_ln_stats(bb, xs1c, W, 0, W, sq, W, ps_stat, rows, onescol, onesrow,
                      ps_b1, ps_b2, "lnz")
        emit_ln_apply(bb, xs1c, W, z, ZW, 0, W, 1, ps_b1, ps_b2)

        def _vmix(V):
            bb.wait(V, "v")
            for j in range(NDT):
                V.tensor_tensor(dslice(sq, j, 0, W, W), dslice(z, j, 1, ZW, ZW),
                                dslice(z, j, 0, W, ZW), ALU.subtract)
            for j in range(NDT):
                V.scalar_tensor_tensor(dslice(xrbf, j, 0, W, W), dslice(sq, j, 0, W, W),
                                       vecb2[:, C_MR + j:C_MR + j + 1],
                                       dslice(z, j, 0, W, ZW), ALU.mult, ALU.add)
            for j in range(NDT):
                ins = V.scalar_tensor_tensor(dslice(xkbf, j, 0, W, W), dslice(sq, j, 0, W, W),
                                             vecb2[:, C_MK + j:C_MK + j + 1],
                                             dslice(z, j, 0, W, ZW), ALU.mult, ALU.add)
            bb.inc(ins, "v")
        bb.V(_vmix)

        for et in range(NDT):
            emit_mm(bb, ps_mm, wk, 1024, xkbf, W, 0, W, et)
            def _vk(V, et=et):
                bb.wait(V, "t")
                ins = V.tensor_scalar_max(scr[:], ps_mm[:, 0:W], 0.0)
                bb.inc(ins, "v")
            bb.V(_vk)
            def _sk(S, et=et):
                bb.wait(S, "v")
                ins = S.activation(out=dslice(k2bf, et, 0, W, W), in_=scr[:], func=AF.Square)
                bb.inc(ins, "s")
            bb.S(_sk)

        for et in range(NDT):
            emit_mm(bb, ps_mm, wv, 1024, k2bf, W, 0, W, et)
            def _vv(V, et=et):
                bb.wait(V, "t")
                ins = V.tensor_copy(dslice(kvsb, et, 0, W, W), ps_mm[:, 0:W])
                bb.inc(ins, "v")
            bb.V(_vv)

        for et in range(NDT):
            emit_mm(bb, ps_mm, wr, 1024, xrbf, W, 0, W, et)
            def _sr(S, et=et):
                bb.wait(S, "t")
                ins = S.activation(out=scr[:], in_=ps_mm[:, 0:W], func=AF.Sigmoid)
                bb.inc(ins, "s")
            bb.S(_sr)
            def _vr(V, et=et):
                bb.wait(V, "s")
                ins = V.tensor_tensor(dslice(kvsb, et, 0, W, W), scr[:],
                                      dslice(kvsb, et, 0, W, W), ALU.mult)
                bb.inc(ins, "v")
            bb.V(_vr)

        emit_ln_stats(bb, kvsb, W, 0, W, sq, W, ps_stat, rows, onescol, onesrow,
                      ps_b1, ps_b2, "lno2")
        emit_ln_apply_res(bb, kvsb, W, xs1c, W, xs1c, W, 0, W, 0, 0, ps_b1, ps_b2, sq, W)

        def _ys2(Y, c=c):
            bb.wait(Y, "v")
            for j in range(NDT):
                bb.inc(Y.dma_start(out=io["xs2_t"][j * 128:(j + 1) * 128, c * W:(c + 1) * W],
                                   in_=dslice(xs1c, j, 0, W, W)), "st", 16)
        bb.Y(_ys2)

        emit_ln_stats(bb, xs1c, W, 0, W, sq, W, ps_stat, rows, onescol, onesrow,
                      ps_b1, ps_b2, "ln4")
        emit_ln_apply(bb, xs1c, W, ln4bf, W, 0, W, 0, ps_b1, ps_b2, mid=(sq, W))

        for p in range(NDT):
            emit_mm(bb, ps_mm, wqkv, 3072, ln4bf, W, 0, W, p)
            def _sq1(S, p=p):
                bb.wait(S, "t")
                S.activation(out=qdn[:], in_=ps_mm[:, 0:W], func=AF.Identity,
                             scale=0.125, bias=qb8[:, p:p + 1])
                ins = S.activation(out=scr[:], in_=ps_mm[:, 0:W], func=AF.Square,
                                   scale=0.125, bias=qb8[:, p:p + 1])
                bb.inc(ins, "s")
            bb.S(_sq1)
            def _tn(T, p=p):
                bb.wait(T, "s")
                ins = T.matmul(ps_misc[0:2, 0:W], halfones[:], scr[:], start=True, stop=True)
                bb.inc(ins, "t")
            bb.T(_tn)
            def _vn(V):
                bb.wait(V, "t")
                ins = V.tensor_copy(nsq[:], ps_misc[0:2, 0:W])
                bb.inc(ins, "v")
            bb.V(_vn)
            def _tc(T, p=p):
                bb.wait(T, "v")
                T.matmul(ps_mm[:, 0:W], blkpm[:], qdn[:], start=True, stop=False)
                ins = T.matmul(ps_mm[:, 0:W], negsel[:], nsq[:], start=False, stop=True)
                bb.inc(ins, "t")
            bb.T(_tc)
            def _vc(V):
                bb.wait(V, "t")
                ins = V.tensor_copy(combsb[:], ps_mm[:, 0:W])
                bb.inc(ins, "v")
            bb.V(_vc)
            for B in range(W // 128):
                def _tt(T, B=B):
                    bb.wait(T, "v")
                    ins = T.transpose(ps_misc[:, 0:128], combsb[:, B * 128:(B + 1) * 128], ident[:])
                    bb.inc(ins, "t")
                bb.T(_tt)
                def _vm3(V, B=B):
                    bb.wait(V, "t")
                    V.reduce_max(negm[:, 0:1], ps_misc[:, 0:64], axis=AX, negate=True)
                    ins = V.reduce_max(negm[:, 1:2], ps_misc[:, 64:128], axis=AX, negate=True)
                    bb.inc(ins, "v")
                bb.V(_vm3)
                def _se(S, B=B):
                    bb.wait(S, "v")
                    S.activation(out=qptm[:, 0:64], in_=ps_misc[:, 0:64], func=AF.Exp,
                                 bias=negm[:, 0:1])
                    ins = S.activation(out=qptm[:, 64:128], in_=ps_misc[:, 64:128], func=AF.Exp,
                                       bias=negm[:, 1:2])
                    bb.inc(ins, "s")
                bb.S(_se)
                def _vq(V, B=B):
                    bb.wait(V, "s")
                    ins = V.tensor_scalar(qptm[:], qptm[:], 0.125, 1e-6, ALU.mult, ALU.add)
                    bb.inc(ins, "v")
                bb.V(_vq)
                def _tq(T, B=B):
                    bb.wait(T, "v")
                    ins = T.transpose(ps_misc[:, 128:256], qptm[:], ident[:])
                    bb.inc(ins, "t")
                bb.T(_tq)
                def _vq2(V, p=p, B=B):
                    bb.wait(V, "t")
                    ins = V.tensor_copy(dslice(qpc, p, B * 128, (B + 1) * 128, W),
                                        ps_misc[:, 128:256])
                    bb.inc(ins, "v")
                bb.V(_vq2)
            emit_mm(bb, ps_mm, wqkv, 3072, ln4bf, W, 0, W, 16 + p)
            def _sv(S, p=p):
                bb.wait(S, "t")
                ins = S.activation(out=scr2[:], in_=ps_mm[:, 0:W], func=AF.Identity,
                                   bias=vecb2[:, C_QB + 16 + p:C_QB + 16 + p + 1])
                bb.inc(ins, "s")
            bb.S(_sv)
            emit_mm(bb, ps_mm, wqkv, 3072, ln4bf, W, 0, W, 8 + p, wait=("v", "s"))
            def _sk2(S, p=p):
                bb.wait(S, "t")
                S.activation(out=kdn[:], in_=ps_mm[:, 0:W], func=AF.Identity,
                             scale=0.125, bias=qb8[:, 8 + p:8 + p + 1])
                ins = S.activation(out=scr[:], in_=ps_mm[:, 0:W], func=AF.Square,
                                   scale=0.125, bias=qb8[:, 8 + p:8 + p + 1])
                bb.inc(ins, "s")
            bb.S(_sk2)
            def _tn2(T, p=p):
                bb.wait(T, "s")
                ins = T.matmul(ps_misc[0:2, 0:W], halfones[:], scr[:], start=True, stop=True)
                bb.inc(ins, "t")
            bb.T(_tn2)
            def _vn2(V):
                bb.wait(V, "t")
                ins = V.tensor_copy(nsq[:], ps_misc[0:2, 0:W])
                bb.inc(ins, "v")
            bb.V(_vn2)
            def _tc2(T, p=p):
                bb.wait(T, "v")
                T.matmul(ps_mm[:, 0:W], blkpm[:], kdn[:], start=True, stop=False)
                ins = T.matmul(ps_mm[:, 0:W], negsel[:], nsq[:], start=False, stop=True)
                bb.inc(ins, "t")
            bb.T(_tc2)
            def _vc2(V):
                bb.wait(V, "t")
                ins = V.tensor_copy(combsb[:], ps_mm[:, 0:W])
                bb.inc(ins, "v")
            bb.V(_vc2)
            for B in range(W // 128):
                def _tt2(T, B=B):
                    bb.wait(T, "v")
                    T.transpose(ps_misc[:, 0:128], combsb[:, B * 128:(B + 1) * 128], ident[:])
                    ins = T.transpose(ps_misc[:, 256:384], scr2[:, B * 128:(B + 1) * 128], ident[:])
                    bb.inc(ins, "t")
                bb.T(_tt2)
                def _vm4(V, B=B):
                    bb.wait(V, "t")
                    V.reduce_max(negm[:, 2:3], ps_misc[:, 0:64], axis=AX, negate=True)
                    V.reduce_max(negm[:, 3:4], ps_misc[:, 64:128], axis=AX, negate=True)
                    ins = V.tensor_copy(vtm[:], ps_misc[:, 256:384])
                    bb.inc(ins, "v")
                bb.V(_vm4)
                def _se2(S, B=B):
                    bb.wait(S, "v")
                    S.activation(out=qptm[:, 0:64], in_=ps_misc[:, 0:64], func=AF.Exp,
                                 bias=negm[:, 2:3])
                    ins = S.activation(out=qptm[:, 64:128], in_=ps_misc[:, 64:128], func=AF.Exp,
                                       bias=negm[:, 3:4])
                    bb.inc(ins, "s")
                bb.S(_se2)
                def _vk2(V, B=B):
                    bb.wait(V, "s")
                    ins = V.tensor_scalar(kptm[:], qptm[:], 0.125, 1e-6, ALU.mult, ALU.add)
                    bb.inc(ins, "v")
                bb.V(_vk2)
                def _tkv(T, p=p, B=B, c=c):
                    bb.wait(T, "v")
                    for s_ in range(2):
                        h = 2 * p + s_
                        T.matmul(ps_kv[0:64, h * 64:(h + 1) * 64],
                                 kptm[:, s_ * 64:(s_ + 1) * 64], vtm[:, s_ * 64:(s_ + 1) * 64],
                                 start=False, stop=False, skip_group_check=True)
                        ins = T.matmul(ps_ks[0:64, h:h + 1],
                                       kptm[:, s_ * 64:(s_ + 1) * 64], onescbf[:],
                                       start=False, stop=False, skip_group_check=True)
                    bb.inc(ins, "t")
                bb.T(_tkv)
        def _yqp(Y, c=c):
            bb.wait(Y, "v")
            for j in range(NDT):
                bb.inc(Y.dma_start(out=io["qp_t"][j * 128:(j + 1) * 128, c * W:(c + 1) * W],
                                   in_=dslice(qpc, j, 0, W, W)), "st", 16)
        bb.Y(_yqp)
        bb.barrier("vtsy")

    def _vkv(V):
        bb.wait(V, "t")
        for h in range(16):
            V.tensor_copy(kvout[0:64, 65 * h:65 * h + 64], ps_kv[0:64, h * 64:(h + 1) * 64])
        ins = V.tensor_copy(kvout[0:64, 64::65], ps_ks[0:64, 0:16])
        bb.inc(ins, "v")
    bb.V(_vkv)
    def _ykv(Y):
        bb.wait(Y, "v")
        bb.inc(Y.dma_start(out=io["kvp"][:], in_=kvout[:]), "st", 16)
        bb.wait(Y, "st")
    bb.Y(_ykv)


def build_B2(n):
    nc = bass.Bass("TRN2", target_bir_lowering=False, detect_race_conditions=False)
    io = {}
    io["xs1_t"] = nc.dram_tensor("xs1_t", [1024, 1 + n], F32, kind="ExternalInput").ap()
    io["WrT"] = nc.dram_tensor("WrT", [1024, 1024], F32, kind="ExternalInput").ap()
    io["WkT"] = nc.dram_tensor("WkT", [1024, 1024], F32, kind="ExternalInput").ap()
    io["WvT"] = nc.dram_tensor("WvT", [1024, 1024], F32, kind="ExternalInput").ap()
    io["qkvT"] = nc.dram_tensor("qkvT", [1024, 3072], F32, kind="ExternalInput").ap()
    io["blkpm"] = nc.dram_tensor("blkpm", [128, 128], F32, kind="ExternalInput").ap()
    io["vecb2"] = nc.dram_tensor("vecb2", [128, 40], F32, kind="ExternalInput").ap()
    io["maskz"] = nc.dram_tensor("maskz", [128, 1], F32, kind="ExternalInput").ap()
    io["consts"] = nc.dram_tensor("consts", [128, 4], F32, kind="ExternalInput").ap()
    io["ident"] = nc.dram_tensor("ident", [128, 128], F32, kind="ExternalInput").ap()
    io["halfones"] = nc.dram_tensor("halfones", [128, 2], F32, kind="ExternalInput").ap()
    io["negsel"] = nc.dram_tensor("negsel", [2, 128], F32, kind="ExternalInput").ap()
    io["xs2_t"] = nc.dram_tensor("xs2_t", [1024, n], F32, kind="ExternalOutput").ap()
    io["qp_t"] = nc.dram_tensor("qp_t", [1024, n], BF16, kind="ExternalOutput").ap()
    io["kvp"] = nc.dram_tensor("kvp", [64, 1040], F32, kind="ExternalOutput").ap()
    with contextlib.ExitStack() as st:
        bb = BB(nc, st)
        emit_stageB2(bb, nc, st, io, n)
    return nc



def emit_stageC(bb, nc, st, io, n, dbg_xs3=False):
    e = st.enter_context
    consts = e(nc.sbuf_tensor("D_consts", [P, 4], F32))
    onescol = e(nc.sbuf_tensor("D_ones", [P, 1], F32))
    onesrow = e(nc.sbuf_tensor("D_onesr", [1, P], F32))
    ones32 = e(nc.sbuf_tensor("D_ones32", [1, P], F32))
    ident = e(nc.sbuf_tensor("D_ident", [P, 128], F32))
    rows = e(nc.sbuf_tensor("D_rows", [1, 4 * 512], F32))
    vecc = e(nc.sbuf_tensor("D_vecc", [P, 8], F32))
    bgrow = e(nc.sbuf_tensor("D_bgrow", [1, 8], F32))
    wgsb = e(nc.sbuf_tensor("D_wg", [P, NDT * 8], F32))
    sel8 = e(nc.sbuf_tensor("D_sel8", [8, 1024], F32))
    PS = [e(nc.psum_tensor(f"D_p{i}", [P, 512], F32)) for i in range(8)]
    ps_stat, ps_b1, ps_b2, ps_mm, ps_misc, ps_u, ps_v, ps_sp = PS

    nc.const_aps.register(F32, -1.0, consts[:, 0:1])
    nc.const_aps.register(F32, 0.125, consts[:, 1:2])
    nc.const_aps.register(F32, 1e-6, consts[:, 2:3])

    xs3_t = nc.dram_tensor("xs3_t", [1024, n], F32,
                           kind=("ExternalOutput" if dbg_xs3 else "Internal")).ap()
    if dbg_xs3:
        dbg_rows = nc.dram_tensor("dbg_rows", [8, 1024], F32, kind="ExternalOutput").ap()
        dbg_oacc = nc.dram_tensor("dbg_oacc", [128, 8 * 1024], F32, kind="ExternalOutput").ap()
        dbg_lg = nc.dram_tensor("dbg_lg", [128, 16], F32, kind="ExternalOutput").ap()

    def _y0(Y):
        for nm, t in [("consts", consts), ("ident", ident), ("vecc", vecc),
                      ("bgrow", bgrow), ("Wg", wgsb), ("sel8", sel8)]:
            bb.inc(Y.dma_start(out=t[:], in_=io[nm][:]), "yd", 16)
    bb.Y(_y0)

    def _v0(V):
        bb.wait(V, "yd")
        V.memset(onescol[:], 1.0)
        V.memset(onesrow[:], 1.0)
        ins = V.memset(ones32[:], 32.0)
        bb.inc(ins, "v")
    bb.V(_v0)

    with contextlib.ExitStack() as st1:
        e1 = st1.enter_context
        W = 512
        qpc = e1(nc.sbuf_tensor("D_qpc", [P, NDT * W], BF16))
        xs2c = e1(nc.sbuf_tensor("D_xs2c", [P, NDT * W], F32))
        prbf = e1(nc.sbuf_tensor("D_prbf", [P, NDT * W], BF16))
        o3 = e1(nc.sbuf_tensor("D_o3", [P, NDT * W], F32))
        sq1 = e1(nc.sbuf_tensor("D_sq1", [P, NDT * W], F32))
        dens = e1(nc.sbuf_tensor("D_dens", [65, W], F32))
        ones64 = e1(nc.sbuf_tensor("D_ones64", [65, 128], F32))
        bcs = e1(nc.sbuf_tensor("D_bcs", [64, W], F32))
        kvsum = e1(nc.sbuf_tensor("D_kvsum", [64, 1040], F32))
        kvbf = e1(nc.sbuf_tensor("D_kvbf", [128, 1040], BF16))
        wproj = e1(nc.sbuf_tensor("D_wproj", [P, NDT * 1024], BF16))

        def _g1(Gp):
            for j in range(NDT):
                bb.inc(Gp.dma_start(out=wproj[:, j * 1024:(j + 1) * 1024],
                                    in_=io["projT"][j * 128:(j + 1) * 128, :]), "gd", 16)
        bb.G(_g1)
        def _y1(Y):
            bb.inc(Y.dma_start(out=kvsum[:], in_=io["kvp_a"][:]), "yd", 16)
        bb.Y(_y1)
        if io.get("kvp_b") is not None:
            kvtmp = e1(nc.sbuf_tensor("D_kvtmp", [64, 1040], F32))
            def _y2(Y):
                bb.inc(Y.dma_start(out=kvtmp[:], in_=io["kvp_b"][:]), "yd", 16)
            bb.Y(_y2)
            def _v1(V):
                bb.wait(V, "yd")
                V.memset(ones64[64:65, :], 1.0)
                V.tensor_tensor(kvsum[:], kvsum[:], kvtmp[:], ALU.add)
                ins = V.tensor_copy(kvbf[0:64, :], kvsum[:])
                bb.inc(ins, "v")
            bb.V(_v1)
        else:
            def _v1(V):
                bb.wait(V, "yd")
                V.memset(ones64[64:65, :], 1.0)
                ins = V.tensor_copy(kvbf[0:64, :], kvsum[:])
                bb.inc(ins, "v")
            bb.V(_v1)
        def _g2(Gp):
            bb.wait(Gp, "v")
            bb.inc(Gp.dma_start(out=kvbf[64:128, :], in_=kvbf[0:64, :]), "gd", 16)
        bb.G(_g2)
        def _tw(T):
            bb.wait(T, "gd")
            bb.wait(T, "yd")
        bb.T(_tw)
        bb.barrier("vtsy")

        for c in range(n // W):
            def _yl(Y, c=c):
                bb.wait(Y, "v")
                for j in range(NDT):
                    bb.inc(Y.dma_start(out=dslice(qpc, j, 0, W, W),
                                       in_=io["qp_t"][j * 128:(j + 1) * 128, c * W:(c + 1) * W]),
                           "yd", 16)
                for j in range(NDT):
                    bb.inc(Y.dma_start(out=dslice(xs2c, j, 0, W, W),
                                       in_=io["xs2_t"][j * 128:(j + 1) * 128, c * W:(c + 1) * W]),
                           "yd", 16)
            bb.Y(_yl)
            def _vw(V):
                bb.wait(V, "yd")
                bb.inc(V.nop(), "v")
            bb.V(_vw)

            for h in range(16):
                def _to(T, h=h):
                    bb.wait(T, "v")
                    hb = 64 * (h % 2)
                    ins = T.matmul(ps_mm[0:65, 0:W], kvbf[hb:hb + 64, 65 * h:65 * h + 65],
                                   qpc[hb:hb + 64, (h // 2) * W:(h // 2 + 1) * W],
                                   start=True, stop=True)
                    bb.inc(ins, "t")
                bb.T(_to)
                def _vd(V, h=h):
                    bb.wait(V, "t")
                    bb.inc(V.tensor_scalar_add(dens[64:65, :], ps_mm[64:65, 0:W], 1e-6), "v")
                    bb.wait(V, "v")
                    ins = V.reciprocal(dens[64:65, :], dens[64:65, :])
                    bb.inc(ins, "v")
                bb.V(_vd)
                def _tb(T, h=h):
                    bb.wait(T, "v")
                    ins = T.matmul(ps_misc[0:64, 0:W], ones64[64:65, 0:64], dens[64:65, :],
                                   start=True, stop=True)
                    bb.inc(ins, "t")
                bb.T(_tb)
                def _vb(V, h=h):
                    bb.wait(V, "t")
                    V.tensor_copy(bcs[:], ps_misc[0:64, 0:W])
                    ins = V.tensor_tensor(
                        prbf[64 * (h % 2):64 * (h % 2) + 64, (h // 2) * W:(h // 2 + 1) * W],
                        ps_mm[0:64, 0:W], bcs[:], ALU.mult)
                    bb.inc(ins, "v")
                bb.V(_vb)

            for et in range(NDT):
                emit_mm(bb, ps_mm, wproj, 1024, prbf, W, 0, W, et)
                def _so(S, et=et):
                    bb.wait(S, "t")
                    ins = S.activation(out=dslice(o3, et, 0, W, W), in_=ps_mm[:, 0:W],
                                       func=AF.Identity, bias=vecc[:, et:et + 1])
                    bb.inc(ins, "s")
                bb.S(_so)

            emit_ln_stats(bb, o3, W, 0, W, sq1, W, ps_stat, rows, onescol, onesrow,
                          ps_b1, ps_b2, "lno3")
            emit_ln_apply_res(bb, o3, W, xs2c, W, xs2c, W, 0, W, 0, 0, ps_b1, ps_b2, sq1, W)

            def _ys(Y, c=c):
                bb.wait(Y, "v")
                for j in range(NDT):
                    bb.inc(Y.dma_start(out=xs3_t[j * 128:(j + 1) * 128, c * W:(c + 1) * W],
                                       in_=dslice(xs2c, j, 0, W, W)), "st", 16)
            bb.Y(_ys)
            bb.barrier("vtsy")

    def _ywait(Y):
        bb.wait(Y, "st")
    bb.Y(_ywait)
    with contextlib.ExitStack() as st2:
        e2 = st2.enter_context
        SP = min(1024, n)
        SW = 512
        NSUB = SP // SW
        ln3bf = e2(nc.sbuf_tensor("D_ln3", [P, NDT * SP], BF16))
        outacc = e2(nc.sbuf_tensor("D_oacc", [P, NDT * SP], F32))
        h1bf = e2(nc.sbuf_tensor("D_h1", [P, 32 * SP], BF16))
        x3c = e2(nc.sbuf_tensor("D_x3c", [P, NDT * SW], F32))
        sqc = e2(nc.sbuf_tensor("D_sqc", [P, NDT * SW], F32))
        lnf = e2(nc.sbuf_tensor("D_lnf", [P, NDT * SW], F32))
        scrA = [e2(nc.sbuf_tensor(f"D_scrA{i}", [P, SW], F32)) for i in range(2)]
        scrB = [e2(nc.sbuf_tensor(f"D_scrB{i}", [P, SW], F32)) for i in range(2)]
        scbsb = e2(nc.sbuf_tensor("D_scb", [P, SP], F32))
        scrows = e2(nc.sbuf_tensor("D_scrows", [8, SP], F32))
        lgsb = e2(nc.sbuf_tensor("D_lg", [P, 8], F32))
        mx8 = e2(nc.sbuf_tensor("D_mx8", [P, 8], F32))
        m1t = e2(nc.sbuf_tensor("D_m1t", [P, 8], F32))
        m2t = e2(nc.sbuf_tensor("D_m2t", [P, 8], F32))
        sc8 = e2(nc.sbuf_tensor("D_sc8", [P, 8], F32))
        wcols = e2(nc.sbuf_tensor("D_wc", [P, 3], F32))
        w1sl = [e2(nc.sbuf_tensor(f"D_w1sl{i}", [P, 1024], BF16)) for i in range(2)]
        w3sl = [e2(nc.sbuf_tensor(f"D_w3sl{i}", [P, 1024], BF16)) for i in range(2)]
        w2sl = [e2(nc.sbuf_tensor(f"D_w2sl{i}", [P, 1024], BF16)) for i in range(2)]

        nsp = n // SP
        for spi in range(nsp):
            base = spi * SP
            for sub in range(NSUB):
                sb = base + sub * SW
                def _yl(Y, sb=sb):
                    bb.wait(Y, "v")
                    for j in range(NDT):
                        bb.inc(Y.dma_start(out=dslice(x3c, j, 0, SW, SW),
                                           in_=xs3_t[j * 128:(j + 1) * 128, sb:sb + SW]),
                               "yd", 16)
                bb.Y(_yl)
                def _vw(V):
                    bb.wait(V, "yd")
                    bb.inc(V.nop(), "v")
                bb.V(_vw)
                emit_ln_stats(bb, x3c, SW, 0, SW, sqc, SW, ps_stat, rows, onescol,
                              onesrow, ps_b1, ps_b2, "ln3m")
                emit_ln_apply(bb, x3c, SW, lnf, SW, 0, SW, 0, ps_b1, ps_b2)
                def _vc(V, sub=sub):
                    bb.wait(V, "v")
                    for j in range(NDT):
                        ins = V.tensor_copy(dslice(ln3bf, j, sub * SW, (sub + 1) * SW, SP),
                                            dslice(lnf, j, 0, SW, SW))
                    bb.inc(ins, "v")
                bb.V(_vc)
                for tt in range(SW // 128):
                    def _tl(T, tt=tt):
                        bb.wait(T, "v")
                        for j in range(NDT):
                            T.matmul(ps_misc[0:128, 0:8],
                                     dslice(lnf, j, tt * 128, (tt + 1) * 128, SW),
                                     wgsb[:, j * 8:(j + 1) * 8],
                                     start=(j == 0), stop=False)
                        ins = T.matmul(ps_misc[0:128, 0:8], onesrow[0:1, :], bgrow[:],
                                       start=False, stop=True)
                        bb.inc(ins, "t")
                    bb.T(_tl)
                    def _vr(V, tt=tt):
                        bb.wait(V, "t")
                        bb.inc(V.tensor_copy(lgsb[:], ps_misc[0:128, 0:8]), "v")
                        bb.wait(V, "v")
                        ins = V.max(mx8[:], lgsb[:])
                        bb.inc(ins, "v")
                    bb.V(_vr)
                    def _vr2(V):
                        bb.wait(V, "v")
                        ins = V.tensor_tensor(wcols[:, 2:3], mx8[:, 0:1], mx8[:, 1:2],
                                              ALU.subtract)
                        bb.inc(ins, "v")
                    bb.V(_vr2)
                    if dbg_xs3 and spi == 0 and sub == 0 and tt == 0:
                        def _ydl(Y):
                            bb.wait(Y, "v")
                            bb.inc(Y.dma_start(out=dbg_lg[:, 0:8], in_=lgsb[:]), "st", 16)
                            bb.inc(Y.dma_start(out=dbg_lg[:, 8:16], in_=mx8[:]), "st", 16)
                        bb.Y(_ydl)
                    def _sw(S):
                        bb.wait(S, "v")
                        S.activation(out=wcols[:, 0:1], in_=wcols[:, 2:3], func=AF.Sigmoid)
                        ins = S.activation(out=wcols[:, 1:2], in_=wcols[:, 2:3],
                                           func=AF.Sigmoid, scale=-1.0)
                        bb.inc(ins, "s")
                    bb.S(_sw)
                    def _vm(V, tt=tt, sub=sub):
                        bb.wait(V, "s")
                        V.tensor_scalar(m1t[:], lgsb[:], mx8[:, 0:1], None, ALU.is_equal)
                        V.tensor_scalar(m2t[:], lgsb[:], mx8[:, 1:2], None, ALU.is_equal)
                        V.tensor_scalar(sc8[:], m1t[:], wcols[:, 0:1], None, ALU.mult)
                        ins = V.scalar_tensor_tensor(sc8[:], m2t[:], wcols[:, 1:2], sc8[:],
                                                     ALU.mult, ALU.add)
                        bb.inc(ins, "v")
                    bb.V(_vm)
                    def _tt2(T, tt=tt):
                        bb.wait(T, "v")
                        ins = T.transpose(ps_sp[0:8, 0:128], sc8[:], ident[:])
                        bb.inc(ins, "t")
                    bb.T(_tt2)
                    def _vt(V, tt=tt, sub=sub):
                        bb.wait(V, "t")
                        ti = sub * (SW // 128) + tt
                        ins = V.tensor_copy(scrows[0:8, ti * 128:(ti + 1) * 128],
                                            ps_sp[0:8, 0:128])
                        bb.inc(ins, "v")
                    bb.V(_vt)

            if dbg_xs3 and spi == 0:
                def _ydr(Y):
                    bb.wait(Y, "v")
                    bb.inc(Y.dma_start(out=dbg_rows[0:8, 0:SP], in_=scrows[:]), "st", 16)
                bb.Y(_ydr)
            for ex in range(8):
                for sub in range(NSUB):
                    def _ts(T, ex=ex, sub=sub):
                        bb.wait(T, "v")
                        ins = T.matmul(ps_sp[0:128, 0:SW], sel8[0:8, ex * 128:(ex + 1) * 128],
                                       scrows[0:8, sub * SW:(sub + 1) * SW],
                                       start=True, stop=True)
                        bb.inc(ins, "t")
                    bb.T(_ts)
                    def _vs(V, sub=sub):
                        bb.wait(V, "t")
                        ins = V.tensor_copy(scbsb[:, sub * SW:(sub + 1) * SW],
                                            ps_sp[0:128, 0:SW])
                        bb.inc(ins, "v")
                    bb.V(_vs)
                uv_vdone = [bb.val("v"), bb.val("v")]
                tslice_done = [bb.val("t"), bb.val("t")]
                for ht in range(32):
                    par = ht % 2
                    def _gw(Gp, ex=ex, ht=ht, par=par, tv=tslice_done[par]):
                        bb.wait(Gp, "t", tv)
                        bb.inc(Gp.dma_start(out=w1sl[par][:], in_=io["W1h"][ex, ht]), f"wd{par}", 16)
                        bb.inc(Gp.dma_start(out=w3sl[par][:], in_=io["W3h"][ex, ht]), f"wd{par}", 16)
                    bb.G(_gw)
                    wd_ready = bb.val(f"wd{par}")
                    UB = [PS[4 * par + 0], PS[4 * par + 1]]
                    VB = [PS[4 * par + 2], PS[4 * par + 3]]
                    def _tu(T, ht=ht, par=par, wd_ready=wd_ready, UB=UB, VB=VB,
                            vv=uv_vdone[par]):
                        bb.wait(T, f"wd{par}", wd_ready)
                        bb.wait(T, "v", vv)
                        for kt in range(NDT):
                            for s2 in range(NSUB):
                                T.matmul(UB[s2][:, 0:SW],
                                         w1sl[par][:, kt * 128:(kt + 1) * 128],
                                         dslice(ln3bf, kt, s2 * SW, (s2 + 1) * SW, SP),
                                         start=(kt == 0), stop=(kt == NDT - 1))
                        for kt in range(NDT):
                            for s2 in range(NSUB):
                                ins = T.matmul(VB[s2][:, 0:SW],
                                               w3sl[par][:, kt * 128:(kt + 1) * 128],
                                               dslice(ln3bf, kt, s2 * SW, (s2 + 1) * SW, SP),
                                               start=(kt == 0), stop=(kt == NDT - 1))
                        bb.inc(ins, "t")
                    bb.T(_tu)
                    tslice_done[par] = bb.val("t")
                    for s2 in range(NSUB):
                        def _ve(V, s2=s2, UB=UB):
                            bb.wait(V, "t")
                            ins = V.tensor_tensor(scrA[s2][:], UB[s2][:, 0:SW],
                                                  scbsb[:, s2 * SW:(s2 + 1) * SW], ALU.mult)
                            bb.inc(ins, "v")
                        bb.V(_ve)
                        def _ss(S, s2=s2):
                            bb.wait(S, "v")
                            ins = S.activation(out=scrB[s2][:], in_=scrA[s2][:], func=AF.Silu)
                            bb.inc(ins, "s")
                        bb.S(_ss)
                        def _ve2(V, ht=ht, s2=s2, VB=VB):
                            bb.wait(V, "s")
                            V.tensor_tensor(scrA[s2][:], scrB[s2][:],
                                            scbsb[:, s2 * SW:(s2 + 1) * SW], ALU.mult)
                            ins = V.tensor_tensor(
                                h1bf[:, ht * SP + s2 * SW: ht * SP + (s2 + 1) * SW],
                                scrA[s2][:], VB[s2][:, 0:SW], ALU.mult)
                            bb.inc(ins, "v")
                        bb.V(_ve2)
                    uv_vdone[par] = bb.val("v")
                w2_tdone = [bb.val("t"), bb.val("t")]
                for s2 in range(NSUB):
                    for kt in range(32):
                        par = kt % 2
                        def _gw2(Gp, ex=ex, kt=kt, par=par, tv=w2_tdone[par]):
                            bb.wait(Gp, "t", tv)
                            bb.inc(Gp.dma_start(
                                out=w2sl[par][:],
                                in_=io["W2"][ex * 4096 + kt * 128: ex * 4096 + (kt + 1) * 128, :]),
                                f"wd{par}", 16)
                        bb.G(_gw2)
                        wd_ready = bb.val(f"wd{par}")
                        def _ty(T, kt=kt, s2=s2, par=par, wd_ready=wd_ready):
                            bb.wait(T, f"wd{par}", wd_ready)
                            if kt == 0:
                                bb.wait(T, "v")
                            for dt in range(NDT):
                                ins = T.matmul(PS[dt][:, 0:SW],
                                               w2sl[par][:, dt * 128:(dt + 1) * 128],
                                               h1bf[:, kt * SP + s2 * SW: kt * SP + (s2 + 1) * SW],
                                               start=(kt == 0), stop=(kt == 31))
                            bb.inc(ins, "t")
                        bb.T(_ty)
                        w2_tdone[par] = bb.val("t")
                    def _vy(V, ex=ex, s2=s2):
                        bb.wait(V, "t")
                        for dt in range(NDT):
                            col = s2 * SW
                            if ex == 0:
                                ins = V.tensor_copy(dslice(outacc, dt, col, col + SW, SP),
                                                    PS[dt][:, 0:SW])
                            else:
                                ins = V.tensor_tensor(dslice(outacc, dt, col, col + SW, SP),
                                                      dslice(outacc, dt, col, col + SW, SP),
                                                      PS[dt][:, 0:SW], ALU.add)
                        bb.inc(ins, "v")
                    bb.V(_vy)

            if dbg_xs3 and spi == 0:
                def _ydo(Y):
                    bb.wait(Y, "v")
                    bb.inc(Y.dma_start(out=dbg_oacc[:, 0:NDT * SP], in_=outacc[:]), "st", 16)
                bb.Y(_ydo)
            for sub in range(NSUB):
                sb = base + sub * SW
                def _yl2(Y, sb=sb):
                    bb.wait(Y, "v")
                    for j in range(NDT):
                        bb.inc(Y.dma_start(out=dslice(x3c, j, 0, SW, SW),
                                           in_=xs3_t[j * 128:(j + 1) * 128, sb:sb + SW]),
                               "yd", 16)
                bb.Y(_yl2)
                def _vw2(V):
                    bb.wait(V, "yd")
                    bb.inc(V.nop(), "v")
                bb.V(_vw2)
                emit_ln_stats(bb, x3c, SW, 0, SW, sqc, SW, ps_stat, rows, onescol,
                              onesrow, ps_b1, ps_b2, "ln3f")
                emit_ln_apply(bb, x3c, SW, lnf, SW, 0, SW, 0, ps_b1, ps_b2)
                def _vmo(V, sub=sub):
                    bb.wait(V, "v")
                    for j in range(NDT):
                        ins = V.tensor_tensor(dslice(lnf, j, 0, SW, SW),
                                              dslice(lnf, j, 0, SW, SW),
                                              dslice(outacc, j, sub * SW, (sub + 1) * SW, SP),
                                              ALU.add)
                    bb.inc(ins, "v")
                bb.V(_vmo)
                emit_ln_stats(bb, lnf, SW, 0, SW, sqc, SW, ps_stat, rows, onescol,
                              onesrow, ps_b1, ps_b2, "lnmo")
                emit_ln_apply_res(bb, lnf, SW, x3c, SW, x3c, SW, 0, SW, 0, 0,
                                  ps_b1, ps_b2, sqc, SW)
                def _yo(Y, sb=sb):
                    bb.wait(Y, "v")
                    for j in range(NDT):
                        bb.inc(Y.dma_start(out=io["xout_t"][j * 128:(j + 1) * 128, sb:sb + SW],
                                           in_=dslice(x3c, j, 0, SW, SW)), "st", 16)
                bb.Y(_yo)
            bb.barrier("vtsyg")

    def _yf(Y):
        bb.wait(Y, "st")
    bb.Y(_yf)


def build_C(n, dbg_xs3=False):
    nc = bass.Bass("TRN2", target_bir_lowering=False, detect_race_conditions=False)
    io = {}
    io["xs2_t"] = nc.dram_tensor("xs2_t", [1024, n], F32, kind="ExternalInput").ap()
    io["qp_t"] = nc.dram_tensor("qp_t", [1024, n], BF16, kind="ExternalInput").ap()
    io["kvp_a"] = nc.dram_tensor("kvp_a", [64, 1040], F32, kind="ExternalInput").ap()
    io["kvp_b"] = nc.dram_tensor("kvp_b", [64, 1040], F32, kind="ExternalInput").ap()
    io["projT"] = nc.dram_tensor("projT", [1024, 1024], F32, kind="ExternalInput").ap()
    io["Wg"] = nc.dram_tensor("Wg", [128, 64], F32, kind="ExternalInput").ap()
    io["bgrow"] = nc.dram_tensor("bgrow", [1, 8], F32, kind="ExternalInput").ap()
    io["sel8"] = nc.dram_tensor("sel8", [8, 1024], F32, kind="ExternalInput").ap()
    io["W1h"] = nc.dram_tensor("W1h", [8, 32, 128, 1024], F32, kind="ExternalInput").ap()
    io["W3h"] = nc.dram_tensor("W3h", [8, 32, 128, 1024], F32, kind="ExternalInput").ap()
    io["W2"] = nc.dram_tensor("W2", [8 * 4096, 1024], F32, kind="ExternalInput").ap()
    io["vecc"] = nc.dram_tensor("vecc", [128, 8], F32, kind="ExternalInput").ap()
    io["ident"] = nc.dram_tensor("ident", [128, 128], F32, kind="ExternalInput").ap()
    io["consts"] = nc.dram_tensor("consts", [128, 4], F32, kind="ExternalInput").ap()
    io["xout_t"] = nc.dram_tensor("xout_t", [1024, n], F32, kind="ExternalOutput").ap()
    with contextlib.ExitStack() as st:
        bb = BB(nc, st)
        emit_stageC(bb, nc, st, io, n, dbg_xs3=dbg_xs3)
    return nc



RG_PAIRS = [[0, 1], [2, 3], [4, 5], [6, 7]]


def emit_fused(bb, nc, st, io, n):
    mk = lambda nm, shp, dt=F32: nc.dram_tensor(nm, shp, dt, kind="Internal").ap()
    x2p_t = mk("x2p_t", [1024, 1 + n])
    g_t = mk("g_t", [1024, 1 + n])
    hfin_d = mk("hfin_d", [128, 8])
    cch_in = mk("cch_in", [128, 8])
    cch_out = mk("cch_out", [128, 8])
    hin_d = mk("hin_d", [128, 8])
    xs1_t = mk("xs1_t", [1024, 1 + n])
    xs2_t = mk("xs2_t", [1024, n])
    qp_t = mk("qp_t", [1024, n], BF16)
    kvp_d = mk("kvp_d", [64, 1040])
    cckv_out = mk("cckv_out", [64, 1040])

    ioA = dict(io)
    ioA.update({"x2p_t": x2p_t, "g_t": g_t, "hfin": hfin_d})
    with contextlib.ExitStack() as sA:
        emit_stageA(bb, nc, sA, ioA, n)
    bb.barrier("vtsyg")

    with contextlib.ExitStack() as sg:
        eg = sg.enter_context
        htmp = eg(nc.sbuf_tensor("G_htmp", [P, 8], F32))
        htmp2 = eg(nc.sbuf_tensor("G_htmp2", [P, 8], F32))
        cmask = eg(nc.sbuf_tensor("G_cmask", [P, 1], F32))
        umask = eg(nc.sbuf_tensor("G_umask", [P, 1], F32))

        def _y(Y):
            bb.wait(Y, "st")
            bb.inc(Y.dma_start(out=htmp[:], in_=hfin_d[:]), "yd", 16)
            bb.inc(Y.dma_start(out=cmask[:], in_=io["cmask"][:]), "yd", 16)
            bb.inc(Y.dma_start(out=umask[:], in_=io["umask"][:]), "yd", 16)
        bb.Y(_y)

        def _v(V):
            bb.wait(V, "yd")
            ins = V.tensor_scalar_mul(htmp2[:], htmp[:], cmask[:, 0:1])
            bb.inc(ins, "v")
        bb.V(_v)

        def _y2(Y):
            bb.wait(Y, "v")
            bb.inc(Y.dma_start(out=cch_in[:], in_=htmp2[:]), "st", 16)
        bb.Y(_y2)

        def _g(Gp):
            bb.wait(Gp, "st")
            ins = Gp.collective_compute("AllReduce", ALU.add, RG_PAIRS,
                                        ins=[cch_in.opt()], outs=[cch_out.opt()])
            bb.inc(ins, "cc", 1)
        bb.G(_g)

        def _y3(Y):
            bb.wait(Y, "cc")
            bb.inc(Y.dma_start(out=htmp[:], in_=cch_out[:]), "yd", 16)
        bb.Y(_y3)

        def _v2(V):
            bb.wait(V, "yd")
            ins = V.tensor_scalar_mul(htmp2[:], htmp[:], umask[:, 0:1])
            bb.inc(ins, "v")
        bb.V(_v2)

        def _y4(Y):
            bb.wait(Y, "v")
            bb.inc(Y.dma_start(out=hin_d[:], in_=htmp2[:]), "st", 16)
            bb.wait(Y, "st")
        bb.Y(_y4)
    bb.barrier("vtsyg")

    ioB1 = dict(io)
    ioB1.update({"x2p_t": x2p_t, "g_t": g_t, "h_in": hin_d, "xs1_t": xs1_t})
    with contextlib.ExitStack() as sB:
        emit_stageB1(bb, nc, sB, ioB1, n)
    def _yw1(Y):
        bb.wait(Y, "st")
    bb.Y(_yw1)
    bb.barrier("vtsyg")

    ioB2 = dict(io)
    ioB2.update({"xs1_t": xs1_t, "xs2_t": xs2_t, "qp_t": qp_t, "kvp": kvp_d})
    with contextlib.ExitStack() as sB2:
        emit_stageB2(bb, nc, sB2, ioB2, n)
    bb.barrier("vtsyg")

    def _gkv(Gp):
        bb.wait(Gp, "st")
        ins = Gp.collective_compute("AllReduce", ALU.add, RG_PAIRS,
                                    ins=[kvp_d.opt()], outs=[cckv_out.opt()])
        bb.inc(ins, "cc", 1)
    bb.G(_gkv)
    def _ykv(Y):
        bb.wait(Y, "cc")
    bb.Y(_ykv)
    bb.barrier("vtsyg")

    ioC = dict(io)
    ioC.update({"xs2_t": xs2_t, "qp_t": qp_t, "kvp_a": cckv_out, "kvp_b": None})
    with contextlib.ExitStack() as sC:
        emit_stageC(bb, nc, sC, ioC, n)


def build_fused(n):
    nc = bass.Bass("TRN2", target_bir_lowering=False, detect_race_conditions=False)
    io = {}
    E = lambda nm, shp, dt=F32: nc.dram_tensor(nm, shp, dt, kind="ExternalInput").ap()
    io["x_t"] = E("x_t", [1024, 3 + n])
    io["pwT"] = E("pwT", [1024, 1024])
    io["gruT"] = E("gruT", [1024, 2048])
    io["veca"] = E("veca", [128, 56])
    io["consts"] = E("consts", [128, 4])
    io["cmask"] = E("cmask", [128, 1])
    io["umask"] = E("umask", [128, 1])
    io["ff1T"] = E("ff1T", [1024, 4096])
    io["ff2T"] = E("ff2T", [4096, 1024])
    io["vecb"] = E("vecb", [128, 48])
    io["WrT"] = E("WrT", [1024, 1024])
    io["WkT"] = E("WkT", [1024, 1024])
    io["WvT"] = E("WvT", [1024, 1024])
    io["qkvT"] = E("qkvT", [1024, 3072])
    io["blkpm"] = E("blkpm", [128, 128])
    io["vecb2"] = E("vecb2", [128, 40])
    io["maskz"] = E("maskz", [128, 1])
    io["ident"] = E("ident", [128, 128])
    io["halfones"] = E("halfones", [128, 2])
    io["negsel"] = E("negsel", [2, 128])
    io["projT"] = E("projT", [1024, 1024])
    io["Wg"] = E("Wg", [128, 64])
    io["bgrow"] = E("bgrow", [1, 8])
    io["sel8"] = E("sel8", [8, 1024])
    io["W1h"] = E("W1h", [8, 32, 128, 1024])
    io["W3h"] = E("W3h", [8, 32, 128, 1024])
    io["W2"] = E("W2", [8 * 4096, 1024])
    io["vecc"] = E("vecc", [128, 8])
    io["xout_t"] = nc.dram_tensor("xout_t", [1024, n], F32, kind="ExternalOutput").ap()
    with contextlib.ExitStack() as st:
        bb = BB(nc, st)
        emit_fused(bb, nc, st, io, n)
    return nc


from concourse.bass_utils import run_bass_kernel_spmd

_N = 4096


def _pk(v, k=8):
    return np.ascontiguousarray(np.asarray(v, np.float32).reshape(k, 128).T)


def kernel(L0_attn_x, L0_attn_conv_dw_w, L0_attn_conv_dw_b, L0_attn_conv_pw_w,
           L0_attn_conv_pw_b, L0_attn_conv_norm_g, L0_attn_gru_norm_g, L0_attn_gru_w,
           L0_attn_ff_norm_g, L0_attn_ff_w1, L0_attn_ff_b1, L0_attn_ff_w2, L0_attn_ff_b2,
           L0_ffn_W_r, L0_ffn_W_k, L0_ffn_W_v, L0_ffn_mu_r, L0_ffn_mu_k,
           L1_attn_qkv_w, L1_attn_qkv_b, L1_attn_proj_w, L1_attn_proj_b,
           L1_attn_projection_matrix, L1_ffn_Wg, L1_ffn_bg, L1_ffn_W1, L1_ffn_W2,
           L1_ffn_W3):
    x = np.asarray(L0_attn_x, np.float32)
    B, T, D = x.shape
    n = _N
    cores = list(range(8))

    x_ts = []
    for c in cores:
        b, hf = c // 2, c % 2
        if hf == 0:
            xa = np.concatenate([np.zeros((3, D), np.float32), x[b, 0:n]], 0)
        else:
            xa = x[b, n - 3:2 * n]
        x_ts.append(np.ascontiguousarray(xa.T))

    consts = np.stack([np.full(128, -1.0), np.full(128, 0.125), np.full(128, 1e-6),
                       np.zeros(128)], 1).astype(np.float32)
    pwT = np.ascontiguousarray(np.asarray(L0_attn_conv_pw_w, np.float32).T)
    gruT = np.ascontiguousarray(np.asarray(L0_attn_gru_w, np.float32).T)
    wdw = np.asarray(L0_attn_conv_dw_w, np.float32)
    veca = np.concatenate([
        _pk(L0_attn_conv_norm_g), _pk(L0_attn_gru_norm_g), _pk(L0_attn_conv_dw_b),
        _pk(wdw[:, 0, 0]), _pk(wdw[:, 0, 1]), _pk(wdw[:, 0, 2]),
        _pk(L0_attn_conv_pw_b)], axis=1).astype(np.float32)
    ff1T = np.ascontiguousarray(np.asarray(L0_attn_ff_w1, np.float32).T)
    ff2T = np.ascontiguousarray(np.asarray(L0_attn_ff_w2, np.float32).T)
    vecb = np.concatenate([_pk(L0_attn_ff_norm_g), _pk(L0_attn_ff_b1, 32),
                           _pk(L0_attn_ff_b2)], axis=1).astype(np.float32)
    WrT = np.ascontiguousarray(np.asarray(L0_ffn_W_r, np.float32).T)
    WkT = np.ascontiguousarray(np.asarray(L0_ffn_W_k, np.float32).T)
    WvT = np.ascontiguousarray(np.asarray(L0_ffn_W_v, np.float32).T)
    qkvT = np.ascontiguousarray(np.asarray(L1_attn_qkv_w, np.float32).T)
    vecb2 = np.concatenate([_pk(L0_ffn_mu_r), _pk(L0_ffn_mu_k),
                            _pk(L1_attn_qkv_b, 24)], axis=1).astype(np.float32)
    pm = np.asarray(L1_attn_projection_matrix, np.float32)
    blkpm = np.zeros((128, 128), np.float32)
    blkpm[0:64, 0:64] = pm.T
    blkpm[64:128, 64:128] = pm.T
    halfones = np.zeros((128, 2), np.float32)
    halfones[0:64, 0] = 1.0
    halfones[64:128, 1] = 1.0
    negsel = np.zeros((2, 128), np.float32)
    negsel[0, 0:64] = -0.5
    negsel[1, 64:128] = -0.5
    ident = np.eye(128, dtype=np.float32)
    mask0 = np.zeros((128, 1), np.float32)
    mask1 = np.ones((128, 1), np.float32)
    projT = np.ascontiguousarray(np.asarray(L1_attn_proj_w, np.float32).T)
    wg = np.asarray(L1_ffn_Wg, np.float32)
    wg_pk = np.ascontiguousarray(wg.reshape(8, 128, 8).transpose(1, 0, 2).reshape(128, 64))
    sel8 = np.zeros((8, 1024), np.float32)
    for e_ in range(8):
        sel8[e_, 128 * e_:128 * (e_ + 1)] = 1.0
    W1m = np.asarray(L1_ffn_W1, np.float32)
    W3m = np.asarray(L1_ffn_W3, np.float32)
    W1h = np.ascontiguousarray(
        W1m.reshape(8, 8, 128, 32, 128).transpose(0, 3, 2, 1, 4).reshape(8, 32, 128, 1024))
    W3h = np.ascontiguousarray(
        W3m.reshape(8, 8, 128, 32, 128).transpose(0, 3, 2, 1, 4).reshape(8, 32, 128, 1024))
    W2f = np.ascontiguousarray(np.asarray(L1_ffn_W2, np.float32).reshape(8 * 4096, 1024))
    bgrow = np.ascontiguousarray(np.asarray(L1_ffn_bg, np.float32).reshape(1, 8))
    vecc = _pk(L1_attn_proj_b)

    nc = build_fused(n)
    im = []
    for c in cores:
        even = (c % 2 == 0)
        im.append({
            "x_t": x_ts[c], "pwT": pwT, "gruT": gruT, "veca": veca, "consts": consts,
            "cmask": (mask1 if even else mask0), "umask": (mask0 if even else mask1),
            "ff1T": ff1T, "ff2T": ff2T, "vecb": vecb,
            "WrT": WrT, "WkT": WkT, "WvT": WvT, "qkvT": qkvT, "blkpm": blkpm,
            "vecb2": vecb2, "maskz": (mask0 if even else mask1), "ident": ident,
            "halfones": halfones, "negsel": negsel,
            "projT": projT, "Wg": wg_pk, "bgrow": bgrow, "sel8": sel8,
            "W1h": W1h, "W3h": W3h, "W2": W2f, "vecc": vecc,
        })
    res = run_bass_kernel_spmd(nc, im, core_ids=cores).results

    out = np.empty((B, T, D), np.float32)
    for c in cores:
        b, hf = c // 2, c % 2
        out[b, hf * n:(hf + 1) * n] = res[c]["xout_t"].T
    return out
